# version 03
# baseline (speedup 1.0000x reference)
import numpy as np
from contextlib import ExitStack

import concourse.bass as bass
import concourse.mybir as mybir
from concourse.bass_utils import run_bass_kernel_spmd

F32 = mybir.dt.float32
BF16 = mybir.dt.bfloat16
AF = mybir.ActivationFunctionType
ALU = mybir.AluOpType
AX = mybir.AxisListType

D = 2048
TPC = 2048
NCH = D // 128
NCORES = 8
RND_C = float(1.5 * 2.0 ** 23)


class Sem:
    def __init__(self, handle):
        self.h = handle
        self.n = 0

    def inc(self, inst, k=1):
        inst.then_inc(self.h, k)
        self.n += k
        return inst


class Chain:
    def __init__(self, sem):
        self.s = sem

    def add(self, inst, k=1):
        self.s.inc(inst, k)
        return inst

    def dep(self, eng):
        eng.wait_ge(self.s.h, self.s.n)


class Prog:
    def __init__(self):
        self.nc = bass.Bass("TRN2", target_bir_lowering=False)
        self.ctx = ExitStack()

    def close(self):
        self.ctx.close()

    def sbuf(self, name, shape, dt):
        return self.ctx.enter_context(self.nc.sbuf_tensor(name, shape, dt))

    def psum(self, name, shape, dt=F32):
        return self.ctx.enter_context(self.nc.psum_tensor(name, shape, dt))

    def sem(self, name):
        return Sem(self.ctx.enter_context(self.nc.semaphore(name)))

    def block(self, name=None):
        if name is not None:
            self._nblk = getattr(self, "_nblk", 0) + 1
            name = f"{name}_{self._nblk}"
        return self.nc.Block(name)


def build_quant_launch():
    P = Prog()
    nc = P.nc
    wgu = nc.dram_tensor("wgu_part", [2048, D], F32, kind="ExternalInput").ap()
    wp = nc.dram_tensor("wp_part", [256, 8192], F32, kind="ExternalInput").ap()
    lgu = nc.dram_tensor("lgu_part", [128, 16 * 16 * 128], BF16,
                         kind="ExternalOutput").ap()
    lp = nc.dram_tensor("lp_part", [128, 2 * 64 * 128], BF16,
                        kind="ExternalOutput").ap()

    sw = [P.sbuf(f"sw{i}", [128, 8192], F32) for i in range(2)]
    sbw = P.sbuf("sbw", [128, 8192], BF16)
    sgs = P.sbuf("sgs", [128, 128], F32)
    ssc = P.sbuf("ssc", [128, 128], BF16)
    srcp = P.sbuf("srcp", [128, 128], F32)
    srq = P.sbuf("srq", [128, 8192], F32)
    sq = P.sbuf("sq", [128, 8192], BF16)
    swh = P.sbuf("swh", [128, 8192], BF16)
    smcol = [P.sbuf(f"smcol{i}", [128, 8192], BF16) for i in range(2)]
    sid = P.sbuf("sid", [128, 128], BF16)
    sones = P.sbuf("sones", [128, 128], BF16)
    ptr = [P.psum(f"ptr{i}", [128, 128], BF16) for i in range(4)]

    ld = [P.sem("ld0"), P.sem("ld1")]
    qd = P.sem("qd")
    td = P.sem("td")
    ed = P.sem("ed")
    fd = P.sem("fd")
    st = [P.sem("st0"), P.sem("st1")]
    gp = P.sem("gp")
    pp = P.sem("pp")

    with P.block() as block:
        @block.gpsimd
        def _(g):
            gp.inc(g.memset(sones[:, :], 1.0))
            gp.inc(g.memset(sid[:, :], 0.0))
            g.wait_ge(gp.h, gp.n)
            gp.inc(g.affine_select(sid[:, :], sones[:, :], [[-1, 128]],
                                   ALU.is_equal, 0.0, base=0,
                                   channel_multiplier=1))

    tiles = []
    for m in range(16):
        tiles.append((wgu, m * 128, D, lgu, m * (16 * 128)))
    for m in range(2):
        tiles.append((wp, m * 128, 8192, lp, m * (64 * 128)))
    ntiles = len(tiles)
    vend = []
    cumK = []
    tot = 0
    for (_, _, IN, _, _) in tiles:
        tot += IN // 128
        cumK.append(tot)

    with P.block() as block:
        @block.vector
        def _(v):
            ch = Chain(pp)
            for i, (wap, r0, IN, _, _) in enumerate(tiles):
                G = IN // 64
                v.wait_ge(ld[i % 2].h, 16 * (i // 2 + 1))
                if i >= 1:
                    v.wait_ge(td.h, cumK[i - 1])
                w = sw[i % 2][:, 0:IN]
                ch.add(v.tensor_copy(sbw[:, 0:IN], w))
                va = sbw[:, 0:IN].rearrange("p (g k) -> p g k", k=64)
                ch.dep(v)
                ch.add(v.tensor_reduce(sgs[:, 0:G].unsqueeze(2), va, AX.X,
                                       ALU.add, apply_absolute_value=True))
                ch.dep(v)
                ch.add(v.tensor_scalar(ssc[:, 0:G], sgs[:, 0:G], 1.0 / 64, 1e-8,
                                       ALU.mult, ALU.max))
                ch.dep(v)
                ch.add(v.reciprocal(srcp[:, 0:G], ssc[:, 0:G]))
                ch.dep(v)
                rb = srcp[:, 0:G].unsqueeze(2).broadcast_to((128, G, 64))
                sqv = sq[:, 0:IN].rearrange("p (g k) -> p g k", k=64)
                ch.add(v.tensor_tensor(sqv, va, rb, ALU.mult))
                ch.dep(v)
                ch.add(v.tensor_scalar(srq[:, 0:IN], sq[:, 0:IN], RND_C, RND_C,
                                       ALU.add, ALU.subtract))
                ch.dep(v)
                ch.add(v.tensor_scalar(sq[:, 0:IN], srq[:, 0:IN], 1.0, -1.0,
                                       ALU.min, ALU.max))
                ch.dep(v)
                scb = ssc[:, 0:G].unsqueeze(2).broadcast_to((128, G, 64))
                ch.add(v.tensor_tensor(swh[:, 0:IN].rearrange("p (g k) -> p g k", k=64),
                                       sq[:, 0:IN].rearrange("p (g k) -> p g k", k=64),
                                       scb, ALU.mult))
                ch.dep(v)
                ch.add(v.tensor_tensor(sq[:, 0:IN], swh[:, 0:IN], sbw[:, 0:IN],
                                       ALU.subtract))
                ch.dep(v)
                ch.add(v.tensor_tensor(swh[:, 0:IN], sq[:, 0:IN],
                                       sbw[:, 0:IN], ALU.add))
                vend.append(pp.n)

        @block.sync
        def _(sync):
            for i, (wap, r0, IN, _, _) in enumerate(tiles):
                if i >= 2:
                    sync.wait_ge(pp.h, vend[i - 2])
                ld[i % 2].inc(sync.dma_start(out=sw[i % 2][:, 0:IN],
                                             in_=wap[r0:r0 + 128, :]), 16)

        @block.tensor
        def _(t):
            nt = 0
            for i, (wap, r0, IN, _, _) in enumerate(tiles):
                K = IN // 128
                t.wait_ge(pp.h, vend[i])
                for k in range(K):
                    if nt >= 4:
                        t.wait_ge(ed.h, nt - 3)
                    td.inc(t.transpose(ptr[nt % 4][:, :],
                                       swh[:, k * 128:(k + 1) * 128],
                                       sid[:, :]))
                    nt += 1

        @block.scalar
        def _(s):
            nt = 0
            for i, (wap, r0, IN, _, _) in enumerate(tiles):
                K = IN // 128
                if i >= 2:
                    s.wait_ge(st[i % 2].h, 16 * ((i - 2) // 2 + 1))
                for k in range(K):
                    s.wait_ge(td.h, nt + 1)
                    ed.inc(s.copy(smcol[i % 2][:, k * 128:(k + 1) * 128],
                                  ptr[nt % 4][:, :]))
                    nt += 1
                fd.inc(s.drain())

        @block.gpsimd
        def _(g):
            for i, (wap, r0, IN, lap, off) in enumerate(tiles):
                K = IN // 128
                g.wait_ge(fd.h, i + 1)
                tot_free = lap.tensor.shape[1]
                dst = bass.AP(lap.tensor, off, [[tot_free, 128], [1, K * 128]])
                st[i % 2].inc(g.dma_start(out=dst,
                                          in_=smcol[i % 2][:, 0:K * 128]), 16)
            g.wait_ge(st[0].h, st[0].n)
            g.wait_ge(st[1].h, st[1].n)

    P.close()
    return P.nc



def _shard_x(x):
    out = []
    for c in range(NCORES):
        b, h = c // 2, c % 2
        xs = x[b, h * TPC:(h + 1) * TPC, :]
        out.append(np.ascontiguousarray(xs.T))
    return out


def _col16(vec):
    return np.ascontiguousarray(vec.reshape(16, 128).T)


_BUILD_CACHE = {}


def _get(name, fn):
    if name not in _BUILD_CACHE:
        _BUILD_CACHE[name] = fn()
    return _BUILD_CACHE[name]


def _run(name, fn, in_maps, outs):
    nc = _get(name, fn)
    res = run_bass_kernel_spmd(nc, in_maps, core_ids=list(range(NCORES)))
    return [{k: res.results[c][k] for k in outs} for c in range(NCORES)]


def kernel(**inputs):
    I = inputs
    x = np.asarray(I["L0_attn_x"], np.float32)
    xs = _shard_x(x)

    wgu = np.ascontiguousarray(np.asarray(I["L0_ffn_w_gate_up"], np.float32))
    wpj = np.ascontiguousarray(np.asarray(I["L0_ffn_w_proj"], np.float32))
    rq = _run("q", build_quant_launch,
              [{"wgu_part": wgu[c * 2048:(c + 1) * 2048],
                "wp_part": wpj[c * 256:(c + 1) * 256]} for c in range(NCORES)],
              ["lgu_part", "lp_part"])
    lgu = np.concatenate([r["lgu_part"] for r in rq], axis=1)
    lp = np.concatenate([r["lp_part"] for r in rq], axis=1)

    wxy = np.asarray(I["L1_attn_W_xy"], np.float32)
    wres = np.asarray(I["L1_attn_W_resid"], np.float32)
    fc1 = np.asarray(I["L1_ffn_fc1_w"], np.float32)
    fc2 = np.asarray(I["L1_ffn_fc2_w"], np.float32)
    shu = np.asarray(I["L1_ffn_shared_up_w"], np.float32)
    shd = np.asarray(I["L1_ffn_shared_down_w"], np.float32)
    w1 = np.asarray(I["L1_ffn_w1"], np.float32)
    w2 = np.asarray(I["L1_ffn_w2"], np.float32)
    cast_in = []
    for c in range(NCORES):
        cast_in.append({
            "wxy": np.ascontiguousarray(wxy[c * 512:(c + 1) * 512]),
            "wres": np.ascontiguousarray(wres[c * 256:(c + 1) * 256]),
            "wfc1": np.ascontiguousarray(fc1[c * 128:(c + 1) * 128]),
            "wfc2": np.ascontiguousarray(fc2[c * 256:(c + 1) * 256]),
            "wshu": np.ascontiguousarray(shu[c * 512:(c + 1) * 512]),
            "wshd": np.ascontiguousarray(shd[c * 256:(c + 1) * 256]),
            "ww1": np.ascontiguousarray(
                w1[c * 4:(c + 1) * 4].reshape(4096, 1024)),
            "ww2": np.ascontiguousarray(
                w2[c * 4:(c + 1) * 4].reshape(4096, 1024)),
        })
    rc = _run("w", build_cast_launch, cast_in,
              ["l" + n for n, _, _, _, _ in CAST_JOBS])
    lw = {n: np.concatenate([r["l" + n] for r in rc], axis=1)
          for n, _, _, _, _ in CAST_JOBS}

    xs = [r["xout"] for r in _run(
        "ffn", build_l0ffn_launch,
        [{"xin": xs[c], "lgu": lgu, "lp": lp} for c in range(NCORES)],
        ["xout"])]

    bqc = _col16(np.asarray(I["L0_attn_bq"], np.float32))
    bkc = _col16(np.asarray(I["L0_attn_bk"], np.float32))
    bvr = np.asarray(I["L0_attn_bv"], np.float32).reshape(1, D)
    r1 = _run("qkv", build_qkv_launch,
              [{"xin": xs[c], "wq": np.asarray(I["L0_attn_Wq"], np.float32),
                "wk": np.asarray(I["L0_attn_Wk"], np.float32),
                "wv": np.asarray(I["L0_attn_Wv"], np.float32),
                "bqc": bqc, "bkc": bkc, "bv": bvr} for c in range(NCORES)],
              ["qT", "kT", "vtm"])

    import ml_dtypes
    bf = ml_dtypes.bfloat16
    j = np.arange(4096)
    boc = _col16(np.asarray(I["L0_attn_bo"], np.float32))
    attn_in = []
    for c in range(NCORES):
        half = c % 2
        qbase = half * TPC
        kg = j - (TPC - qbase)
        ab = np.where(kg >= 0, -(4095.0 - kg), -1e30).astype(np.float32)
        abias = np.ascontiguousarray(ab.reshape(32, 128).T)
        qramp = (4095.0 - (qbase + np.arange(TPC))).astype(
            np.float32).reshape(1, TPC)
        kfull = np.zeros((D, 2 * TPC), bf)
        vfull = np.zeros((2 * TPC, D), bf)
        if half == 1:
            kfull[:, 0:TPC] = r1[c - 1]["kT"]
            vfull[0:TPC] = r1[c - 1]["vtm"]
        kfull[:, TPC:] = r1[c]["kT"]
        vfull[TPC:] = r1[c]["vtm"]
        attn_in.append({"qT": r1[c]["qT"], "kfull": kfull, "vfull": vfull,
                        "abias": abias, "qramp": qramp,
                        "wo": np.asarray(I["L0_attn_Wo"], np.float32),
                        "boc": boc, "xin": xs[c]})
    xs = [r["xout"] for r in _run("attn", build_attn_launch, attn_in,
                                  ["xout"])]

    xs = [r["xout"] for r in _run(
        "ffn", build_l0ffn_launch,
        [{"xin": xs[c], "lgu": lgu, "lp": lp} for c in range(NCORES)],
        ["xout"])]

    gw = np.asarray(I["L1_ffn_gate_w"], np.float32)
    gbrow = np.asarray(I["L1_ffn_gate_bias"], np.float32).reshape(1, 32)
    moe_in = [{"xin": xs[c], "gw": gw, "gbrow": gbrow,
               "lwfc1": lw["wfc1"], "lwfc2": lw["wfc2"],
               "lwshu": lw["wshu"], "lwshd": lw["wshd"],
               "lw1": lw["ww1"], "lw2": lw["ww2"]} for c in range(NCORES)]
    xs = [r["xout"] for r in _run("moe", build_moe_launch, moe_in, ["xout"])]

    rg1 = _run("xy", build_xy_launch,
               [{"xin": xs[c], "lwxy": lw["wxy"]} for c in range(NCORES)],
               ["xrT", "yT"])
    conv_w = np.asarray(I["L1_attn_conv_w"], np.float32)
    cwc = np.zeros((128, 64), np.float32)
    for ch in range(16):
        for i in range(4):
            cwc[:, 4 * ch + i] = conv_w[i, ch * 128:(ch + 1) * 128]
    cbc = _col16(np.asarray(I["L1_attn_conv_b"], np.float32))
    igb = np.ascontiguousarray(np.asarray(I["L1_attn_ig_b"], np.float32).T)
    agb = np.ascontiguousarray(np.asarray(I["L1_attn_ag_b"], np.float32).T)
    apc = _col16(np.asarray(I["L1_attn_a_param"], np.float32))
    scan_in = []
    for c in range(NCORES):
        halo = np.zeros((D, 3), np.float32)
        if c % 2 == 1:
            halo = np.ascontiguousarray(rg1[c - 1]["xrT"][:, TPC - 3:TPC])
        scan_in.append({"xrT": rg1[c]["xrT"], "halo": halo, "cwc": cwc,
                        "cbc": cbc,
                        "igw": np.asarray(I["L1_attn_ig_w"], np.float32),
                        "agw": np.asarray(I["L1_attn_ag_w"], np.float32),
                        "igbc": igb, "agbc": agb, "apc": apc})
    rg2 = _run("scan", build_scan_launch, scan_in, ["hT", "acT"])
    g3_in = []
    for c in range(NCORES):
        carry = np.zeros((128, 16), np.float32)
        if c % 2 == 1:
            carry = np.ascontiguousarray(
                rg2[c - 1]["hT"][:, TPC - 1].reshape(16, 128).T)
        g3_in.append({"hT": rg2[c]["hT"], "acT": rg2[c]["acT"],
                      "yT": rg1[c]["yT"], "carryc": carry,
                      "lwres": lw["wres"], "xin": xs[c]})
    xs = [r["xout"] for r in _run("gout", build_gout_launch, g3_in, ["xout"])]

    moe_in = [{"xin": xs[c], "gw": gw, "gbrow": gbrow,
               "lwfc1": lw["wfc1"], "lwfc2": lw["wfc2"],
               "lwshu": lw["wshu"], "lwshd": lw["wshd"],
               "lw1": lw["ww1"], "lw2": lw["ww2"]} for c in range(NCORES)]
    xs = [r["xout"] for r in _run("moe", build_moe_launch, moe_in, ["xout"])]

    out = np.empty((4, 2 * TPC, D), np.float32)
    for c in range(NCORES):
        b, h = c // 2, c % 2
        out[b, h * TPC:(h + 1) * TPC, :] = xs[c].T
    return out


class Ctx:
    def __init__(self, P, QC=512):
        self.QC = QC
        self.xq2 = [P.sbuf(f"c_xq2{i}", [128, QC], F32) for i in range(2)]
        _sx = P.sbuf("c_sxq2", [128, QC], BF16)
        self.sxq2 = [_sx, _sx]
        self.mb = P.sbuf("c_mb", [128, QC], F32)
        self.rb = P.sbuf("c_rb", [128, QC], F32)
        self.hr0 = P.sbuf("c_hr0", [1, QC], F32)
        self.hr1 = P.sbuf("c_hr1", [1, QC], F32)
        self.hr2 = P.sbuf("c_hr2", [1, QC], F32)
        self.onesb = P.sbuf("c_onesb", [128, 1], BF16)
        self.onesf = P.sbuf("c_onesf", [128, 1], F32)
        self.onesr = P.sbuf("c_onesr", [1, 128], F32)
        self.onesbr2 = P.sbuf("c_onesbr2", [1, 128], BF16)
        self.ld = P.sem("c_ld")
        self.ldn = [P.sem(f"c_ldn{i}") for i in range(2)]
        self.vch = P.sem("c_vch")
        self.sch = P.sem("c_sch")
        self.tch = P.sem("c_tch")
        self.gch = P.sem("c_gch")

    def emit_consts(self, P):
        with P.block("ctxc") as block:
            @block.gpsimd
            def _(g):
                self.gch.inc(g.memset(self.onesb[:, :], 1.0))
                self.gch.inc(g.memset(self.onesf[:, :], 1.0))
                self.gch.inc(g.memset(self.onesr[:, :], 1.0))
                self.gch.inc(g.memset(self.onesbr2[:, :], 1.0))
                g.wait_ge(self.gch.h, self.gch.n)


def emit_ln(P, C, xin, xn, ps, save_m=None, save_r=None):
    QC = C.QC
    NQ = TPC // QC
    vch, sch, tch = C.vch, C.sch, C.tch
    ld, ldn = C.ld, C.ldn
    xq2, sxq2 = C.xq2, C.sxq2
    hr0, hr1, hr2, mb, rb = C.hr0, C.hr1, C.hr2, C.mb, C.rb

    with P.block("ln1") as block:
        S0, T0 = sch.n, tch.n
        LN0 = [ldn[0].n, ldn[1].n]

        @block.scalar
        def _(s):
            for ch in range(NCH):
                for qf in range(NQ):
                    i = ch * NQ + qf
                    s.wait_ge(ldn[i % 2].h, LN0[i % 2] + 16 * (i // 2 + 1))
                    if i >= 1:
                        s.wait_ge(tch.h, T0 + i)
                    sch.inc(s.square(sxq2[i % 2][:, :], xq2[i % 2][:, :]))

        @block.sync
        def _(sync):
            for ch in range(NCH):
                for qf in range(NQ):
                    i = ch * NQ + qf
                    if i >= 2:
                        sync.wait_ge(tch.h, T0 + i - 1)
                    ldn[i % 2].inc(sync.dma_start(
                        out=xq2[i % 2][:, :],
                        in_=bass.AP(xin.tensor,
                                    (ch * 128) * TPC + qf * QC,
                                    [[TPC, 128], [1, QC]])), 16)

        @block.tensor
        def _(t):
            for ch in range(NCH):
                for qf in range(NQ):
                    i = ch * NQ + qf
                    t.wait_ge(sch.h, S0 + i + 1)
                    t.matmul(ps[qf][0:1, :], C.onesf[:, :],
                             xq2[i % 2][:, :],
                             start=(ch == 0), stop=(ch == NCH - 1))
                    mm = t.matmul(ps[4 + qf][0:1, :], C.onesb[:, :],
                                  sxq2[i % 2][:, :],
                                  start=(ch == 0), stop=(ch == NCH - 1))
                    tch.inc(mm)

    with P.block("ln2") as block:
        S0, T0 = sch.n, tch.n
        LN0 = [ldn[0].n, ldn[1].n]
        xn_done = {}
        bc_ready = []
        var_ready = []

        @block.vector
        def _(v):
            for qf in range(NQ):
                vch.inc(v.tensor_scalar_mul(hr0[0:1, :],
                                            ps[qf][0:1, :], 1.0 / D))
                vch.inc(v.tensor_scalar_mul(hr1[0:1, :],
                                            ps[4 + qf][0:1, :], 1.0 / D))
                v.wait_ge(vch.h, vch.n)
                vch.inc(v.tensor_tensor(hr2[0:1, :], hr0[0:1, :],
                                        hr0[0:1, :], ALU.mult))
                v.wait_ge(vch.h, vch.n)
                vch.inc(v.tensor_tensor(hr2[0:1, :], hr1[0:1, :],
                                        hr2[0:1, :], ALU.subtract))
                v.wait_ge(vch.h, vch.n)
                vch.inc(v.tensor_scalar_add(hr2[0:1, :], hr2[0:1, :], 1e-5))
                var_ready.append(vch.n)
                v.wait_ge(sch.h, S0 + qf + 1)
                vch.inc(v.reciprocal(hr1[0:1, :], hr2[0:1, :]))
                bc_ready.append(vch.n)
                v.wait_ge(vch.h, vch.n)
                if save_m is not None:
                    vch.inc(v.tensor_copy(
                        save_m[0:1, qf * QC:(qf + 1) * QC], hr0[0:1, :]))
                if save_r is not None:
                    vch.inc(v.tensor_copy(
                        save_r[0:1, qf * QC:(qf + 1) * QC], hr1[0:1, :]))
                if xn is None:
                    v.wait_ge(vch.h, vch.n)
                    continue
                v.wait_ge(tch.h, T0 + qf + 1)
                vch.inc(v.tensor_copy(mb[:, :], ps[qf][:, :]))
                vch.inc(v.tensor_copy(rb[:, :], ps[4 + qf][:, :]))
                v.wait_ge(vch.h, vch.n)
                for ch in range(NCH):
                    i = qf * NCH + ch
                    v.wait_ge(ldn[ch % 2].h,
                              LN0[ch % 2] + 16 * (qf * 8 + ch // 2 + 1))
                    b = xq2[ch % 2][:, :]
                    vch.inc(v.tensor_tensor(b, b, mb[:, :], ALU.subtract))
                    v.wait_ge(vch.h, vch.n)
                    vch.inc(v.tensor_tensor(
                        xn[ch][:, qf * QC:(qf + 1) * QC], b, rb[:, :],
                        ALU.mult))
                    xn_done[i] = vch.n

        @block.scalar
        def _(s):
            for qf in range(NQ):
                s.wait_ge(vch.h, var_ready[qf])
                sch.inc(s.sqrt(hr2[0:1, :], hr2[0:1, :]))

        if xn is not None:
            @block.tensor
            def _(t):
                for qf in range(NQ):
                    t.wait_ge(vch.h, bc_ready[qf])
                    t.matmul(ps[qf][:, :], C.onesr[:, :], hr0[0:1, :],
                             start=True, stop=True)
                    tch.inc(t.matmul(ps[4 + qf][:, :], C.onesr[:, :],
                                     hr1[0:1, :], start=True, stop=True))

            @block.sync
            def _(sync):
                for qf in range(NQ):
                    for ch in range(NCH):
                        i = qf * NCH + ch
                        if i >= 2:
                            sync.wait_ge(vch.h, xn_done[i - 2])
                        ldn[ch % 2].inc(sync.dma_start(
                            out=xq2[ch % 2][:, :],
                            in_=bass.AP(xin.tensor,
                                        (ch * 128) * TPC + qf * QC,
                                        [[TPC, 128], [1, QC]])), 16)


def emit_xnq(P, C, xin, xnq, qf, mrowb, rrowb, ps):
    QC = C.QC
    vch, tch, ldn = C.vch, C.tch, C.ldn
    with P.block("xnq") as block:
        T0, V0 = tch.n, vch.n
        LD0 = [ldn[0].n, ldn[1].n]

        @block.tensor
        def _(t):
            t.matmul(ps[6][:, :], C.onesbr2[:, :],
                     mrowb[0:1, qf * QC:(qf + 1) * QC], start=True, stop=True)
            tch.inc(t.matmul(ps[7][:, :], C.onesbr2[:, :],
                             rrowb[0:1, qf * QC:(qf + 1) * QC],
                             start=True, stop=True))

        @block.vector
        def _(v):
            v.wait_ge(tch.h, T0 + 1)
            vch.inc(v.tensor_copy(C.mb[:, :], ps[6][:, :]))
            vch.inc(v.tensor_copy(C.rb[:, :], ps[7][:, :]))
            v.wait_ge(vch.h, vch.n)
            for ch in range(NCH):
                v.wait_ge(ldn[ch % 2].h, LD0[ch % 2] + 16 * (ch // 2 + 1))
                b = C.xq2[ch % 2][:, :]
                vch.inc(v.tensor_tensor(b, b, C.mb[:, :], ALU.subtract))
                v.wait_ge(vch.h, vch.n)
                vch.inc(v.tensor_tensor(xnq[ch][:, :], b, C.rb[:, :],
                                        ALU.mult))

        @block.sync
        def _(sync):
            for ch in range(NCH):
                if ch >= 2:
                    sync.wait_ge(vch.h, V0 + 2 + 2 * (ch - 2) + 2)
                ldn[ch % 2].inc(sync.dma_start(
                    out=C.xq2[ch % 2][:, :],
                    in_=bass.AP(xin.tensor, (ch * 128) * TPC + qf * QC,
                                [[TPC, 128], [1, QC]])), 16)


def build_l0ffn_launch():
    QC = 512
    NQ = TPC // QC
    MH = 64
    P = Prog()
    nc = P.nc
    xin = nc.dram_tensor("xin", [D, TPC], F32, kind="ExternalInput").ap()
    lgu = nc.dram_tensor("lgu", [128, 128 * 16 * 128], BF16,
                         kind="ExternalInput").ap()
    lp = nc.dram_tensor("lp", [128, 16 * 64 * 128], BF16,
                        kind="ExternalInput").ap()
    xout = nc.dram_tensor("xout", [D, TPC], F32, kind="ExternalOutput").ap()

    C = Ctx(P, QC)
    xn = [P.sbuf(f"xn{i}", [128, TPC], BF16) for i in range(NCH)]
    hh = [P.sbuf(f"hh{i}", [128, QC], BF16) for i in range(MH)]
    sg = [P.sbuf(f"sg{i}", [128, QC], F32) for i in range(2)]
    hq = [P.sbuf(f"hq{i}", [128, QC], BF16) for i in range(2)]
    gcol = [P.sbuf(f"gcol{i}", [128, 16 * 128], BF16) for i in range(4)]
    pcol = [P.sbuf(f"pcol{i}", [128, 32 * 128], BF16) for i in range(2)]
    rb05 = P.sbuf("rb05", [128, QC], F32)
    yt = [P.sbuf(f"yt{i}", [128, QC], F32) for i in range(2)]
    xr = [P.sbuf(f"xr{i}", [128, QC], F32) for i in range(2)]
    ps = [P.psum(f"ps{i}", [128, QC], F32) for i in range(8)]

    ldr = [P.sem(f"ldr{i}") for i in range(2)]
    lw = [P.sem(f"lw{i}") for i in range(2)]
    lwp = [P.sem(f"lwp{i}") for i in range(2)]
    stc = [P.sem(f"stc{i}") for i in range(2)]

    C.emit_consts(P)
    emit_ln(P, C, xin, xn, ps)

    for qc in range(NQ):
        qs = slice(qc * QC, (qc + 1) * QC)

        with P.block() as block:
            T0, S0, V0 = C.tch.n, C.sch.n, C.vch.n
            LWB = [lw[0].n, lw[1].n]
            gate_done, up_done, hs_done = {}, {}, {}
            tc = T0
            for mh in range(MH):
                if mh >= 2:
                    tc += 1; hs_done[mh - 2] = tc
                tc += 1; gate_done[mh] = tc
                tc += 1; up_done[mh] = tc
            tc += 1; hs_done[MH - 2] = tc
            tc += 1; hs_done[MH - 1] = tc
            silu_done, sq_done = {}, {}
            sc = S0
            for mh in range(MH):
                sc += 1; silu_done[mh] = sc
                if mh >= 1:
                    sc += 1; sq_done[mh - 1] = sc
            sc += 1; sq_done[MH - 1] = sc
            h_done = {mh: V0 + mh + 1 for mh in range(MH)}

            @block.tensor
            def _(t):
                def hsmm(j):
                    t.wait_ge(C.sch.h, sq_done[j])
                    C.tch.inc(t.matmul(ps[4][0:1, :], C.onesb[:, :],
                                     hq[j % 2][:, :],
                                     start=(j == 0), stop=(j == MH - 1)))
                for mh in range(MH):
                    if mh >= 2:
                        hsmm(mh - 2)
                    t.wait_ge(lw[mh % 2].h, LWB[mh % 2] + 32 * (mh // 2 + 1))
                    if mh >= 2:
                        t.wait_ge(C.sch.h, silu_done[mh - 2])
                        t.wait_ge(C.vch.h, h_done[mh - 2])
                    gb = gcol[2 * (mh % 2)]
                    ub = gcol[2 * (mh % 2) + 1]
                    pg = ps[mh % 2]
                    pu = ps[2 + mh % 2]
                    for k in range(NCH):
                        mm = t.matmul(pg[:, :], gb[:, k * 128:(k + 1) * 128],
                                      xn[k][:, qs], start=(k == 0),
                                      stop=(k == NCH - 1))
                        if k == NCH - 1:
                            C.tch.inc(mm)
                    for k in range(NCH):
                        mm = t.matmul(pu[:, :], ub[:, k * 128:(k + 1) * 128],
                                      xn[k][:, qs], start=(k == 0),
                                      stop=(k == NCH - 1))
                        if k == NCH - 1:
                            C.tch.inc(mm)
                hsmm(MH - 2)
                hsmm(MH - 1)

            @block.scalar
            def _(s):
                def sq(j):
                    s.wait_ge(C.vch.h, h_done[j])
                    if j >= 2:
                        s.wait_ge(C.tch.h, hs_done[j - 2])
                    C.sch.inc(s.square(hq[j % 2][:, :], hh[j][:, :]))
                for mh in range(MH):
                    s.wait_ge(C.tch.h, gate_done[mh])
                    if mh >= 2:
                        s.wait_ge(C.vch.h, h_done[mh - 2])
                    C.sch.inc(s.activation(sg[mh % 2][:, :], ps[mh % 2][:, :],
                                         AF.Silu))
                    if mh >= 1:
                        sq(mh - 1)
                sq(MH - 1)

            @block.vector
            def _(v):
                for mh in range(MH):
                    v.wait_ge(C.sch.h, silu_done[mh])
                    v.wait_ge(C.tch.h, up_done[mh])
                    C.vch.inc(v.tensor_tensor(hh[mh][:, :], sg[mh % 2][:, :],
                                            ps[2 + mh % 2][:, :], ALU.mult))

            @block.sync
            def _(sync):
                for mh in range(MH):
                    if mh >= 2:
                        sync.wait_ge(C.tch.h, up_done[mh - 2])
                    lw[mh % 2].inc(sync.dma_start(
                        out=gcol[2 * (mh % 2)][:, :],
                        in_=bass.AP(lgu.tensor, mh * 2048,
                                    [[128 * 16 * 128, 128], [1, 2048]])), 16)
                    lw[mh % 2].inc(sync.dma_start(
                        out=gcol[2 * (mh % 2) + 1][:, :],
                        in_=bass.AP(lgu.tensor, (64 + mh) * 2048,
                                    [[128 * 16 * 128, 128], [1, 2048]])), 16)

        with P.block() as block:
            @block.vector
            def _(v):
                C.vch.inc(v.tensor_scalar_mul(C.hr0[0:1, :], ps[4][0:1, :],
                                            1.0 / 8192))
                v.wait_ge(C.vch.h, C.vch.n)
                C.vch.inc(v.tensor_scalar_add(C.hr0[0:1, :], C.hr0[0:1, :],
                                            1.1920928955078125e-07))

            @block.scalar
            def _(s):
                s.wait_ge(C.vch.h, C.vch.n)
                C.sch.inc(s.sqrt(C.hr0[0:1, :], C.hr0[0:1, :]))

            @block.vector
            def _(v):
                v.wait_ge(C.sch.h, C.sch.n)
                C.vch.inc(v.reciprocal(C.hr1[0:1, :], C.hr0[0:1, :]))
                v.wait_ge(C.vch.h, C.vch.n)
                C.vch.inc(v.tensor_scalar_mul(C.hr1[0:1, :], C.hr1[0:1, :], 0.5))

            @block.tensor
            def _(t):
                t.wait_ge(C.vch.h, C.vch.n)
                C.tch.inc(t.matmul(ps[7][:, :], C.onesr[:, :], C.hr1[0:1, :],
                                 start=True, stop=True))

            @block.vector
            def _(v):
                v.wait_ge(C.tch.h, C.tch.n)
                C.vch.inc(v.tensor_copy(rb05[:, :], ps[7][:, :]))

        with P.block() as block:
            TC0, VC0 = C.tch.n, C.vch.n
            LPB = [lwp[i].n for i in range(2)]
            LRB = [ldr[0].n, ldr[1].n]
            STB = [stc[0].n, stc[1].n]
            half_done = {L: TC0 + L + 1 for L in range(32)}
            ytmul = {j: VC0 + 3 * j + 1 for j in range(16)}
            ytadd = {j: VC0 + 3 * j + 2 for j in range(16)}
            ytflush = {j: VC0 + 3 * j + 3 for j in range(16)}

            @block.tensor
            def _(t):
                for j in range(16):
                    if j >= 2:
                        t.wait_ge(C.vch.h, ytmul[j - 2])
                    for half in range(2):
                        L = 2 * j + half
                        t.wait_ge(lwp[L % 2].h,
                                  LPB[L % 2] + 16 * (L // 2 + 1))
                        for kk in range(32):
                            k = half * 32 + kk
                            mm = t.matmul(ps[5 + j % 2][:, :],
                                          pcol[L % 2][:, kk * 128:(kk + 1) * 128],
                                          hh[k][:, :],
                                          start=(k == 0), stop=(k == MH - 1))
                            if kk == 31:
                                C.tch.inc(mm)

            @block.vector
            def _(v):
                for j in range(16):
                    v.wait_ge(C.tch.h, half_done[2 * j + 1])
                    if j >= 2:
                        v.wait_ge(stc[j % 2].h,
                                  STB[j % 2] + 16 * ((j - 2) // 2 + 1))
                    elif qc > 0:
                        v.wait_ge(stc[j % 2].h, STB[j % 2])
                    C.vch.inc(v.tensor_tensor(yt[j % 2][:, :],
                                            ps[5 + j % 2][:, :],
                                            rb05[:, :], ALU.mult))
                    v.wait_ge(ldr[j % 2].h, LRB[j % 2] + 16 * (j // 2 + 1))
                    v.wait_ge(C.vch.h, C.vch.n)
                    C.vch.inc(v.tensor_tensor(yt[j % 2][:, :], yt[j % 2][:, :],
                                            xr[j % 2][:, :], ALU.add))
                    C.vch.inc(v.drain())

            @block.sync
            def _(sync):
                for j in range(16):
                    for half in range(2):
                        L = 2 * j + half
                        if L >= 2:
                            sync.wait_ge(C.tch.h, half_done[L - 2])
                        lwp[L % 2].inc(sync.dma_start(
                            out=pcol[L % 2][:, :],
                            in_=bass.AP(lp.tensor,
                                        j * (64 * 128) + half * (32 * 128),
                                        [[16 * 64 * 128, 128], [1, 32 * 128]])),
                            16)
                    if j >= 2:
                        sync.wait_ge(C.vch.h, ytadd[j - 2])
                    ldr[j % 2].inc(sync.dma_start(
                        out=xr[j % 2][:, :],
                        in_=bass.AP(xin.tensor, (j * 128) * TPC + qc * QC,
                                    [[TPC, 128], [1, QC]])), 16)

            @block.gpsimd
            def _(g):
                for j in range(16):
                    g.wait_ge(C.vch.h, ytflush[j])
                    stc[j % 2].inc(g.dma_start(
                        out=bass.AP(xout.tensor, (j * 128) * TPC + qc * QC,
                                    [[TPC, 128], [1, QC]]),
                        in_=yt[j % 2][:, :]), 16)

    with P.block() as block:
        @block.gpsimd
        def _(g):
            g.wait_ge(stc[0].h, stc[0].n)
            g.wait_ge(stc[1].h, stc[1].n)
    P.close()
    return P.nc


def build_qkv_launch():
    QC = 512
    NQ = TPC // QC
    P = Prog()
    nc = P.nc
    xin = nc.dram_tensor("xin", [D, TPC], F32, kind="ExternalInput").ap()
    wq = nc.dram_tensor("wq", [D, D], F32, kind="ExternalInput").ap()
    wk = nc.dram_tensor("wk", [D, D], F32, kind="ExternalInput").ap()
    wv = nc.dram_tensor("wv", [D, D], F32, kind="ExternalInput").ap()
    bqc = nc.dram_tensor("bqc", [128, 16], F32, kind="ExternalInput").ap()
    bkc = nc.dram_tensor("bkc", [128, 16], F32, kind="ExternalInput").ap()
    bv = nc.dram_tensor("bv", [1, D], F32, kind="ExternalInput").ap()
    qT = nc.dram_tensor("qT", [D, TPC], BF16, kind="ExternalOutput").ap()
    kT = nc.dram_tensor("kT", [D, TPC], BF16, kind="ExternalOutput").ap()
    vtm = nc.dram_tensor("vtm", [TPC, D], BF16, kind="ExternalOutput").ap()

    C = Ctx(P, QC)
    xn = [P.sbuf(f"xn{i}", [128, TPC], BF16) for i in range(NCH)]
    wsl = [P.sbuf(f"wsl{i}", [128, 16 * 512], BF16) for i in range(2)]
    bcols = P.sbuf("bcols", [128, 32], F32)
    bvrow = P.sbuf("bvrow", [1, D], F32)
    bvb = P.sbuf("bvb", [128, QC], F32)
    ot = [P.sbuf(f"ot{i}", [128, TPC], BF16) for i in range(2)]
    ps = [P.psum(f"ps{i}", [128, QC], F32) for i in range(8)]

    lwl = [P.sem(f"lwl{i}") for i in range(2)]
    lbb = P.sem("lbb")
    stc = [P.sem(f"stc{i}") for i in range(2)]

    C.emit_consts(P)
    emit_ln(P, C, xin, xn, ps)

    vch, sch, tch, gch = C.vch, C.sch, C.tch, C.gch
    SCALE_Q = 0.08838834764831845

    with P.block() as block:
        @block.sync
        def _(sync):
            lbb.inc(sync.dma_start(out=bcols[:, 16:32], in_=bkc), 16)
            lbb.inc(sync.dma_start(out=bvrow[0:1, :], in_=bv), 16)
            lbb.inc(sync.dma_start(out=bvb[:, 0:16], in_=bqc), 16)

        @block.vector
        def _(v):
            v.wait_ge(lbb.h, 48)
            vch.inc(v.tensor_scalar_mul(bcols[:, 0:16], bvb[:, 0:16],
                                        SCALE_Q))

    for pi, (wap, outap, b0, scl) in enumerate(
            [(wq, qT, 0, SCALE_Q), (wk, kT, 16, 1.0)]):
        with P.block() as block:
            T0 = tch.n
            S0 = sch.n
            LW0 = [lwl[0].n, lwl[1].n]
            ST0 = [stc[0].n, stc[1].n]

            @block.gpsimd
            def _(g):
                for jg in range(4):
                    if jg >= 2:
                        g.wait_ge(tch.h, T0 + 16 * (jg - 1))
                    for k in range(NCH):
                        lwl[jg % 2].inc(g.dma_start(
                            out=wsl[jg % 2][:, k * 512:(k + 1) * 512],
                            in_=wap[k * 128:(k + 1) * 128,
                                    jg * 512:(jg + 1) * 512]), 16)

            @block.tensor
            def _(t):
                for jg in range(4):
                    t.wait_ge(lwl[jg % 2].h,
                              LW0[jg % 2] + 16 * 16 * (jg // 2 + 1))
                    for jj in range(4):
                        j = jg * 4 + jj
                        for qf in range(NQ):
                            pb = ps[(j * NQ + qf) % 2]
                            if j * NQ + qf >= 2:
                                i2 = j * NQ + qf - 2
                                t.wait_ge(sch.h,
                                          S0 + (i2 // NQ) * (NQ + 1)
                                          + (i2 % NQ) + 1)
                            for k in range(NCH):
                                mm = t.matmul(
                                    pb[:, :],
                                    wsl[jg % 2][:, k * 512 + jj * 128:
                                                k * 512 + (jj + 1) * 128],
                                    xn[k][:, qf * QC:(qf + 1) * QC],
                                    start=(k == 0), stop=(k == NCH - 1))
                            tch.inc(mm)

            @block.scalar
            def _(s):
                for j in range(16):
                    if j >= 2:
                        s.wait_ge(stc[j % 2].h,
                                  ST0[j % 2] + 16 * ((j - 2) // 2 + 1))
                    elif pi == 1:
                        s.wait_ge(stc[j % 2].h, ST0[j % 2])
                    for qf in range(NQ):
                        s.wait_ge(tch.h, T0 + j * NQ + qf + 1)
                        sch.inc(s.activation(
                            ot[j % 2][:, qf * QC:(qf + 1) * QC],
                            ps[(j * NQ + qf) % 2][:, :], AF.Identity,
                            bias=bcols[:, b0 + j:b0 + j + 1], scale=scl))
                    sch.inc(s.drain())

            @block.sync
            def _(sync):
                for j in range(16):
                    sync.wait_ge(sch.h, S0 + (j + 1) * (NQ + 1))
                    stc[j % 2].inc(sync.dma_start(
                        out=outap[j * 128:(j + 1) * 128, :],
                        in_=ot[j % 2][:, :]), 16)

    with P.block() as block:
        T0 = tch.n
        V0 = vch.n
        LW0 = [lwl[0].n, lwl[1].n]
        ST0 = [stc[0].n, stc[1].n]
        bvb_done = {ov: V0 + ov * 33 + 1 for ov in range(4)}
        vt_done = {}
        for ov in range(4):
            for tokb in range(16):
                vt_done[(ov, tokb)] = V0 + ov * 33 + 1 + 2 * (tokb + 1)

        @block.gpsimd
        def _(g):
            for ov in range(4):
                if ov >= 2:
                    g.wait_ge(tch.h, T0 + 17 * (ov - 1))
                for k in range(NCH):
                    lwl[ov % 2].inc(g.dma_start(
                        out=wsl[ov % 2][:, k * 512:(k + 1) * 512],
                        in_=wv[k * 128:(k + 1) * 128,
                               ov * 512:(ov + 1) * 512]), 16)

        @block.tensor
        def _(t):
            for ov in range(4):
                if ov >= 1:
                    t.wait_ge(vch.h, bvb_done[ov - 1])
                tch.inc(t.matmul(ps[7][:, :], C.onesr[:, :],
                                 bvrow[0:1, ov * QC:(ov + 1) * QC],
                                 start=True, stop=True))
                t.wait_ge(lwl[ov % 2].h,
                          LW0[ov % 2] + 16 * 16 * (ov // 2 + 1))
                for tokb in range(16):
                    i = ov * 16 + tokb
                    pb = ps[i % 2]
                    if i >= 2:
                        t.wait_ge(vch.h, vt_done[((i - 2) // 16,
                                                  (i - 2) % 16)])
                    for k in range(NCH):
                        mm = t.matmul(
                            pb[:, :],
                            xn[k][:, tokb * 128:(tokb + 1) * 128],
                            wsl[ov % 2][:, k * 512:(k + 1) * 512],
                            start=(k == 0), stop=(k == NCH - 1))
                    tch.inc(mm)

        @block.vector
        def _(v):
            for ov in range(4):
                v.wait_ge(tch.h, T0 + ov * 17 + 1)
                vch.inc(v.tensor_copy(bvb[:, :], ps[7][:, :]))
                v.wait_ge(vch.h, vch.n)
                for tokb in range(16):
                    i = ov * 16 + tokb
                    v.wait_ge(tch.h, T0 + ov * 17 + tokb + 2)
                    if i >= 2:
                        v.wait_ge(stc[i % 2].h,
                                  ST0[i % 2] + 16 * ((i - 2) // 2 + 1))
                    elif True:
                        v.wait_ge(stc[i % 2].h, ST0[i % 2])
                    vch.inc(v.tensor_tensor(ot[i % 2][:, 0:QC],
                                            ps[i % 2][:, :], bvb[:, :],
                                            ALU.add))
                    vch.inc(v.drain())

        @block.sync
        def _(sync):
            for ov in range(4):
                for tokb in range(16):
                    i = ov * 16 + tokb
                    sync.wait_ge(vch.h, vt_done[(ov, tokb)])
                    stc[i % 2].inc(sync.dma_start(
                        out=bass.AP(vtm.tensor,
                                    (tokb * 128) * D + ov * QC,
                                    [[D, 128], [1, QC]]),
                        in_=ot[i % 2][:, 0:QC]), 16)

    with P.block() as block:
        @block.gpsimd
        def _(g):
            g.wait_ge(stc[0].h, stc[0].n)
            g.wait_ge(stc[1].h, stc[1].n)
    P.close()
    return P.nc


ALIBI_SLOPES = [0.7071067811865476 ** (i + 1) for i in range(16)]


def build_attn_launch():
    QC = 512
    NQ = TPC // QC
    NH = 16
    P = Prog()
    nc = P.nc
    qT = nc.dram_tensor("qT", [D, TPC], BF16, kind="ExternalInput").ap()
    kfull = nc.dram_tensor("kfull", [D, 2 * TPC], BF16,
                           kind="ExternalInput").ap()
    vfull = nc.dram_tensor("vfull", [2 * TPC, D], BF16,
                           kind="ExternalInput").ap()
    abias = nc.dram_tensor("abias", [128, 32], F32, kind="ExternalInput").ap()
    qramp = nc.dram_tensor("qramp", [1, TPC], F32, kind="ExternalInput").ap()
    wo = nc.dram_tensor("wo", [D, D], F32, kind="ExternalInput").ap()
    boc = nc.dram_tensor("boc", [128, 16], F32, kind="ExternalInput").ap()
    xin = nc.dram_tensor("xin", [D, TPC], F32, kind="ExternalInput").ap()
    xout = nc.dram_tensor("xout", [D, TPC], F32, kind="ExternalOutput").ap()

    kbuf = [P.sbuf(f"kbuf{i}", [128, 2 * TPC], BF16) for i in range(2)]
    qbuf = [P.sbuf(f"qbuf{i}", [128, TPC], BF16) for i in range(2)]
    vbuf = [P.sbuf(f"vbuf{i}", [128, 32 * 128], BF16) for i in range(2)]
    ksq = P.sbuf("ksq", [128, 2 * TPC], BF16)
    qsq = P.sbuf("qsq", [128, TPC], BF16)
    oh = [P.sbuf(f"oh{i}", [128, TPC], BF16) for i in range(NH)]
    ptile = [P.sbuf(f"ptile{i}", [128, QC], BF16) for i in range(2)]
    masks = [P.sbuf(f"mask{i}", [128, QC], F32) for i in range(4)]
    sab = P.sbuf("sab", [128, 32], F32)
    abh = P.sbuf("abh", [128, 32], F32)
    sqr = P.sbuf("sqr", [1, TPC], F32)
    nsrow = [P.sbuf(f"nsrow{i}", [1, QC], BF16) for i in range(2)]
    rbt = P.sbuf("rbt", [128, QC], F32)
    drow = P.sbuf("drow", [1, QC], F32)
    trow = P.sbuf("trow", [1, 4], F32)
    kmx = P.sbuf("kmx", [1, 16], F32)
    qmx = P.sbuf("qmx", [1, 64], F32)
    negb = P.sbuf("negb", [1, 64], F32)
    sboc = P.sbuf("sboc", [128, 16], F32)
    wsl = [P.sbuf(f"wsl{i}", [128, 16 * 512], BF16) for i in range(2)]
    yt = [P.sbuf(f"yt{i}", [128, QC], F32) for i in range(2)]
    xr = [P.sbuf(f"xr{i}", [128, QC], F32) for i in range(2)]
    onesbr = P.sbuf("onesbr", [1, 128], BF16)
    onesb = P.sbuf("onesb", [128, 1], BF16)
    onesr = P.sbuf("onesr", [1, 128], F32)
    ps = [P.psum(f"ps{i}", [128, QC], F32) for i in range(8)]

    lkq = [P.sem(f"lkq{i}") for i in range(2)]
    lsm = P.sem("lsm")
    lwl = [P.sem(f"lwl{i}") for i in range(2)]
    ldr = [P.sem(f"ldr{i}") for i in range(2)]
    vch = P.sem("vch")
    sch = P.sem("sch")
    tch = P.sem("tch")
    gch = P.sem("gch")
    stc = [P.sem(f"stc{i}") for i in range(2)]

    with P.block() as block:
        @block.gpsimd
        def _(g):
            gch.inc(g.memset(onesbr[:, :], 1.0))
            gch.inc(g.memset(onesb[:, :], 1.0))
            gch.inc(g.memset(onesr[:, :], 1.0))
            for r in range(4):
                gch.inc(g.memset(masks[r][:, :], 0.0))
            g.wait_ge(gch.h, gch.n)
            for r in range(4):
                gch.inc(g.affine_select(masks[r][:, :], masks[r][:, :],
                                        [[1, QC]], ALU.is_ge, -1e30,
                                        base=-r * 128, channel_multiplier=-1))
            g.wait_ge(gch.h, gch.n)

        @block.sync
        def _(sync):
            lsm.inc(sync.dma_start(out=sab[:, :], in_=abias), 16)
            lsm.inc(sync.dma_start(out=sqr[0:1, :], in_=qramp), 16)
            lsm.inc(sync.dma_start(out=sboc[:, :], in_=boc), 16)
            sync.wait_ge(lsm.h, lsm.n)

    with P.block() as block:
        sq_done = {h: 3 * (h + 1) for h in range(NH)}
        seg_mm = {}
        tc = tch.n
        order = []
        for h in range(NH):
            for seg in range(12):
                tc += 1
                seg_mm[(h, seg)] = tc
                order.append((h, seg))
        red_done = {}
        vc = vch.n
        for h in range(NH):
            for seg in range(12):
                vc += 2
                red_done[(h, seg)] = vc
            vc += 1

        @block.sync
        def _(sync):
            for h in range(NH):
                if h >= 2:
                    sync.wait_ge(tch.h, seg_mm[(h - 2, 11)])
                lkq[h % 2].inc(sync.dma_start(
                    out=kbuf[h % 2][:, 0:TPC],
                    in_=kfull[h * 128:(h + 1) * 128, 0:TPC]), 16)
                lkq[h % 2].inc(sync.dma_start(
                    out=kbuf[h % 2][:, TPC:2 * TPC],
                    in_=kfull[h * 128:(h + 1) * 128, TPC:2 * TPC]), 16)
                lkq[h % 2].inc(sync.dma_start(
                    out=qbuf[h % 2][:, :],
                    in_=qT[h * 128:(h + 1) * 128, :]), 16)

        @block.scalar
        def _(s):
            for h in range(NH):
                s.wait_ge(lkq[h % 2].h, 48 * (h // 2 + 1))
                if h >= 1:
                    s.wait_ge(tch.h, seg_mm[(h - 1, 11)])
                sch.inc(s.square(ksq[:, 0:TPC], kbuf[h % 2][:, 0:TPC]))
                sch.inc(s.square(ksq[:, TPC:], kbuf[h % 2][:, TPC:]))
                sch.inc(s.square(qsq[:, :], qbuf[h % 2][:, :]))

        @block.tensor
        def _(t):
            for ii, (h, seg) in enumerate(order):
                t.wait_ge(sch.h, sq_done[h])
                if ii >= 1:
                    t.wait_ge(vch.h, red_done[order[ii - 1]])
                src = ksq[:, seg * QC:(seg + 1) * QC] if seg < 8 \
                    else qsq[:, (seg - 8) * QC:(seg - 7) * QC]
                tch.inc(t.matmul(ps[5][0:1, :], onesb[:, :], src,
                                 start=True, stop=True))

        @block.vector
        def _(v):
            for h in range(NH):
                for seg in range(12):
                    v.wait_ge(tch.h, seg_mm[(h, seg)])
                    vch.inc(v.reduce_max(trow[0:1, 0:1], ps[5][0:1, :],
                                         AX.X))
                    v.wait_ge(vch.h, vch.n)
                    if seg < 8:
                        if seg == 0:
                            vch.inc(v.tensor_copy(kmx[0:1, h:h + 1],
                                                  trow[0:1, 0:1]))
                        else:
                            vch.inc(v.tensor_tensor(kmx[0:1, h:h + 1],
                                                    kmx[0:1, h:h + 1],
                                                    trow[0:1, 0:1], ALU.max))
                    else:
                        c = 4 * h + seg - 8
                        vch.inc(v.tensor_copy(qmx[0:1, c:c + 1],
                                              trow[0:1, 0:1]))
                v.wait_ge(vch.h, vch.n)
                vch.inc(v.tensor_scalar(qmx[0:1, 4 * h:4 * h + 4],
                                        qmx[0:1, 4 * h:4 * h + 4],
                                        kmx[0:1, h:h + 1], None, ALU.mult))

    with P.block() as block:
        @block.scalar
        def _(s):
            sch.inc(s.sqrt(negb[0:1, :], qmx[0:1, :]))

        @block.vector
        def _(v):
            v.wait_ge(sch.h, sch.n)
            vch.inc(v.tensor_scalar_mul(negb[0:1, :], negb[0:1, :], -1.0))

    ps_guard = {0: 0, 1: 0}
    pt_guard = {0: 0, 1: 0}
    ns_guard = {0: 0, 1: 0}
    ps4_guard = 0
    drow_guard = 0
    LB0 = [lkq[0].n, lkq[1].n]

    for h in range(NH):
        slope = ALIBI_SLOPES[h]
        with P.block() as block:
            T0, S0, V0 = tch.n, sch.n, vch.n
            nkb = [20 + 4 * qc for qc in range(NQ)]

            r1_done, dens_done, bcast_done = {}, {}, {}
            tc = T0
            for qc in range(NQ):
                for kb in range(nkb[qc]):
                    tc += 1
                    r1_done[(qc, kb)] = tc
                    if kb >= 1:
                        tc += 1
                        dens_done[(qc, kb - 1)] = tc
                tc += 1
                dens_done[(qc, nkb[qc] - 1)] = tc
                tc += 1
                bcast_done[qc] = tc
            head_mm_end = tc
            exp_done = {}
            sc = S0
            for qc in range(NQ):
                for kb in range(nkb[qc]):
                    sc += 1
                    exp_done[(qc, kb)] = sc
            nsrow_done, mask_done = {}, {}
            recip_done, rbt_done, oevac_done = {}, {}, {}
            vc = V0 + 1
            for qc in range(NQ):
                vc += 1
                nsrow_done[qc] = vc
                for kb in range(nkb[qc] - 4, nkb[qc]):
                    vc += 1
                    mask_done[(qc, kb)] = vc
                vc += 1
                recip_done[qc] = vc
                vc += 1
                rbt_done[qc] = vc
                vc += 1
                oevac_done[qc] = vc

            g_ps = dict(ps_guard)
            g_pt = dict(pt_guard)
            g_ns = dict(ns_guard)
            g_ps4 = ps4_guard
            g_drow = drow_guard
            for qc in range(NQ):
                for kb in range(nkb[qc]):
                    ps_guard[kb % 2] = exp_done[(qc, kb)]
                    pt_guard[kb % 2] = dens_done[(qc, kb)]
                ns_guard[qc % 2] = r1_done[(qc, nkb[qc] - 1)]
                ps4_guard = rbt_done[qc]
                drow_guard = bcast_done[qc]

            exp_ps_wait = {}
            pv_pt_wait = {}
            lp = dict(g_ps)
            lq = dict(g_pt)
            for qc in range(NQ):
                for kb in range(nkb[qc]):
                    exp_ps_wait[(qc, kb)] = lq[kb % 2]
                    pv_pt_wait[(qc, kb)] = lp[kb % 2]
                    lp[kb % 2] = exp_done[(qc, kb)]
                    lq[kb % 2] = dens_done[(qc, kb)]
            mm_ps_wait = {}
            lp = dict(g_ps)
            for qc in range(NQ):
                for kb in range(nkb[qc]):
                    mm_ps_wait[(qc, kb)] = lp[kb % 2]
                    lp[kb % 2] = exp_done[(qc, kb)]

            @block.sync
            def _(sync):
                def loads(hh):
                    lkq[hh % 2].inc(sync.dma_start(
                        out=kbuf[hh % 2][:, 0:TPC],
                        in_=kfull[hh * 128:(hh + 1) * 128, 0:TPC]), 16)
                    lkq[hh % 2].inc(sync.dma_start(
                        out=kbuf[hh % 2][:, TPC:],
                        in_=kfull[hh * 128:(hh + 1) * 128, TPC:]), 16)
                    lkq[hh % 2].inc(sync.dma_start(
                        out=qbuf[hh % 2][:, :],
                        in_=qT[hh * 128:(hh + 1) * 128, :]), 16)
                    for kb in range(32):
                        lkq[hh % 2].inc(sync.dma_start(
                            out=vbuf[hh % 2][:, kb * 128:(kb + 1) * 128],
                            in_=vfull[kb * 128:(kb + 1) * 128,
                                      hh * 128:(hh + 1) * 128]), 16)
                if h == 0:
                    loads(0)
                    loads(1)
                if h < NH - 2:
                    sync.wait_ge(tch.h, head_mm_end)
                    loads(h + 2)

            @block.vector
            def _(v):
                v.wait_ge(sch.h, S0)
                vch.inc(v.tensor_scalar_mul(abh[:, :], sab[:, :], slope))
                for qc in range(NQ):
                    nb = negb[0:1, 4 * h + qc:4 * h + qc + 1]
                    v.wait_ge(tch.h, g_ns[qc % 2])
                    vch.inc(v.tensor_scalar(
                        nsrow[qc % 2][0:1, :],
                        sqr[0:1, qc * QC:(qc + 1) * QC],
                        slope, nb, ALU.mult, ALU.add))
                    for kb in range(nkb[qc] - 4, nkb[qc]):
                        r = kb - 16 - 4 * qc
                        v.wait_ge(tch.h, r1_done[(qc, kb)])
                        vch.inc(v.tensor_tensor(ps[kb % 2][:, :],
                                                ps[kb % 2][:, :],
                                                masks[r][:, :], ALU.add))
                    v.wait_ge(tch.h, dens_done[(qc, nkb[qc] - 1)])
                    if qc == 0:
                        v.wait_ge(tch.h, g_drow)
                    vch.inc(v.reciprocal(drow[0:1, :], ps[3][0:1, :]))
                    v.wait_ge(vch.h, vch.n)
                    v.wait_ge(tch.h, bcast_done[qc])
                    vch.inc(v.tensor_copy(rbt[:, :], ps[4][:, :]))
                    v.wait_ge(vch.h, vch.n)
                    vch.inc(v.tensor_tensor(
                        oh[h][:, qc * QC:(qc + 1) * QC],
                        ps[2][:, :], rbt[:, :], ALU.mult))

            @block.scalar
            def _(s):
                s.wait_ge(vch.h, V0 + 1)
                for qc in range(NQ):
                    for kb in range(nkb[qc]):
                        if kb >= nkb[qc] - 4:
                            s.wait_ge(vch.h, mask_done[(qc, kb)])
                        else:
                            s.wait_ge(tch.h, r1_done[(qc, kb)])
                        if exp_ps_wait[(qc, kb)]:
                            s.wait_ge(tch.h, exp_ps_wait[(qc, kb)])
                        sch.inc(s.activation(ptile[kb % 2][:, :],
                                             ps[kb % 2][:, :], AF.Exp,
                                             bias=abh[:, kb:kb + 1]))

            @block.tensor
            def _(t):
                t.wait_ge(lkq[h % 2].h, LB0[h % 2] + 560 * (h // 2 + 1))
                def pv(qc, j, last):
                    t.wait_ge(sch.h, exp_done[(qc, j)])
                    t.matmul(ps[2][:, :],
                             vbuf[h % 2][:, j * 128:(j + 1) * 128],
                             ptile[j % 2][:, :],
                             start=(j == 0), stop=last)
                    tch.inc(t.matmul(ps[3][0:1, :], onesb[:, :],
                                     ptile[j % 2][:, :],
                                     start=(j == 0), stop=last))
                for qc in range(NQ):
                    t.wait_ge(vch.h, nsrow_done[qc])
                    if qc >= 1:
                        t.wait_ge(vch.h, oevac_done[qc - 1])
                        t.wait_ge(vch.h, recip_done[qc - 1])
                    elif g_ps4 or True:
                        t.wait_ge(vch.h, V0)
                    for kb in range(nkb[qc]):
                        if mm_ps_wait[(qc, kb)]:
                            t.wait_ge(sch.h, mm_ps_wait[(qc, kb)])
                        t.matmul(ps[kb % 2][:, :],
                                 kbuf[h % 2][:, kb * 128:(kb + 1) * 128],
                                 qbuf[h % 2][:, qc * QC:(qc + 1) * QC],
                                 start=True, stop=False)
                        tch.inc(t.matmul(ps[kb % 2][:, :], onesbr[:, :],
                                         nsrow[qc % 2][0:1, :],
                                         start=False, stop=True))
                        if kb >= 1:
                            pv(qc, kb - 1, kb - 1 == nkb[qc] - 1)
                    pv(qc, nkb[qc] - 1, True)
                    t.wait_ge(vch.h, recip_done[qc])
                    if qc == 0:
                        t.wait_ge(vch.h, g_ps4)
                    else:
                        t.wait_ge(vch.h, rbt_done[qc - 1])
                    tch.inc(t.matmul(ps[4][:, :], onesr[:, :],
                                     drow[0:1, :], start=True, stop=True))

    with P.block() as block:
        T0, S0, V0 = tch.n, sch.n, vch.n
        LW0 = [lwl[0].n, lwl[1].n]
        LR0 = [ldr[0].n, ldr[1].n]
        ST0 = [stc[0].n, stc[1].n]
        mm_done = {}
        tc = T0
        for j in range(16):
            for qf in range(NQ):
                tc += 1
                mm_done[(j, qf)] = tc
        ytm = {}
        vc = V0
        for j in range(16):
            for qf in range(NQ):
                vc += 2
                ytm[(j, qf)] = vc

        @block.gpsimd
        def _(g):
            for jg in range(4):
                if jg >= 2:
                    g.wait_ge(tch.h, T0 + 16 * (jg - 1))
                for k in range(NCH):
                    lwl[jg % 2].inc(g.dma_start(
                        out=wsl[jg % 2][:, k * 512:(k + 1) * 512],
                        in_=wo[k * 128:(k + 1) * 128,
                               jg * 512:(jg + 1) * 512]), 16)

        @block.tensor
        def _(t):
            for jg in range(4):
                t.wait_ge(lwl[jg % 2].h,
                          LW0[jg % 2] + 16 * 16 * (jg // 2 + 1))
                for jj in range(4):
                    j = jg * 4 + jj
                    for qf in range(NQ):
                        i = j * NQ + qf
                        if i >= 2:
                            i2 = i - 2
                            t.wait_ge(vch.h, ytm[(i2 // NQ, i2 % NQ)])
                        for k in range(NCH):
                            mm = t.matmul(
                                ps[i % 2][:, :],
                                wsl[jg % 2][:, k * 512 + jj * 128:
                                            k * 512 + (jj + 1) * 128],
                                oh[k][:, qf * QC:(qf + 1) * QC],
                                start=(k == 0), stop=(k == NCH - 1))
                        tch.inc(mm)

        @block.vector
        def _(v):
            for j in range(16):
                for qf in range(NQ):
                    i = j * NQ + qf
                    v.wait_ge(tch.h, mm_done[(j, qf)])
                    v.wait_ge(ldr[i % 2].h, LR0[i % 2] + 16 * (i // 2 + 1))
                    if i >= 2:
                        v.wait_ge(stc[i % 2].h,
                                  ST0[i % 2] + 16 * ((i - 2) // 2 + 1))
                    vch.inc(v.scalar_tensor_tensor(
                        yt[i % 2][:, :], ps[i % 2][:, :],
                        sboc[:, j:j + 1], xr[i % 2][:, :],
                        ALU.add, ALU.add))
                    vch.inc(v.drain())

        @block.sync
        def _(sync):
            for j in range(16):
                for qf in range(NQ):
                    i = j * NQ + qf
                    if i >= 2:
                        i2 = i - 2
                        sync.wait_ge(vch.h, ytm[(i2 // NQ, i2 % NQ)] - 1)
                    ldr[i % 2].inc(sync.dma_start(
                        out=xr[i % 2][:, :],
                        in_=bass.AP(xin.tensor, (j * 128) * TPC + qf * QC,
                                    [[TPC, 128], [1, QC]])), 16)
                    sync.wait_ge(vch.h, ytm[(j, qf)])
                    stc[i % 2].inc(sync.dma_start(
                        out=bass.AP(xout.tensor, (j * 128) * TPC + qf * QC,
                                    [[TPC, 128], [1, QC]]),
                        in_=yt[i % 2][:, :]), 16)

    with P.block() as block:
        @block.gpsimd
        def _(g):
            g.wait_ge(stc[0].h, stc[0].n)
            g.wait_ge(stc[1].h, stc[1].n)
    P.close()
    return P.nc


CAST_JOBS = [
    ("wxy", 512, 2048, 4, 16),
    ("wres", 256, 2048, 2, 16),
    ("wfc1", 128, 2048, 1, 16),
    ("wfc2", 256, 1024, 2, 8),
    ("wshu", 512, 2048, 4, 16),
    ("wshd", 256, 4096, 2, 32),
    ("ww1", 4096, 1024, 4 * 8, 8),
    ("ww2", 4096, 1024, 4 * 8, 8),
]


def build_cast_launch():
    P = Prog()
    nc = P.nc
    ins = {}
    outs = {}
    for name, rows, IN, n_m, K in CAST_JOBS:
        ins[name] = nc.dram_tensor(name, [rows, IN], F32,
                                   kind="ExternalInput").ap()
        outs[name] = nc.dram_tensor("l" + name, [128, n_m * K * 128], BF16,
                                    kind="ExternalOutput").ap()

    sbw = [P.sbuf(f"sbw{i}", [128, 4096], BF16) for i in range(2)]
    smcol = [P.sbuf(f"smcol{i}", [128, 4096], BF16) for i in range(2)]
    sid = P.sbuf("sid", [128, 128], BF16)
    sones = P.sbuf("sones", [128, 128], BF16)
    ptr = [P.psum(f"ptr{i}", [128, 128], BF16) for i in range(4)]

    ld = [P.sem(f"ld{i}") for i in range(2)]
    td = P.sem("td")
    ed = P.sem("ed")
    fd = P.sem("fd")
    st = [P.sem(f"st{i}") for i in range(2)]
    gp = P.sem("gp")

    with P.block() as block:
        @block.gpsimd
        def _(g):
            gp.inc(g.memset(sones[:, :], 1.0))
            gp.inc(g.memset(sid[:, :], 0.0))
            g.wait_ge(gp.h, gp.n)
            gp.inc(g.affine_select(sid[:, :], sones[:, :], [[-1, 128]],
                                   ALU.is_equal, 0.0, base=0,
                                   channel_multiplier=1))
            g.wait_ge(gp.h, gp.n)

    tiles = []
    for name, rows, IN, n_m, K in CAST_JOBS:
        for m in range(n_m):
            tiles.append((ins[name], m * 128, IN, outs[name], m * K * 128, K))
    cumK = []
    tot = 0
    for (_, _, IN, _, _, K) in tiles:
        tot += K
        cumK.append(tot)

    with P.block() as block:
        @block.gpsimd
        def _(g):
            for i, (wap, r0, IN, _, _, K) in enumerate(tiles):
                if i >= 2:
                    g.wait_ge(td.h, cumK[i - 1])
                ld[i % 2].inc(g.dma_start(out=sbw[i % 2][:, 0:IN],
                                          in_=wap[r0:r0 + 128, :]), 16)

        @block.tensor
        def _(t):
            nt = 0
            for i, (wap, r0, IN, _, _, K) in enumerate(tiles):
                t.wait_ge(ld[i % 2].h, 16 * (i // 2 + 1))
                for k in range(K):
                    if nt >= 4:
                        t.wait_ge(ed.h, nt - 3)
                    td.inc(t.transpose(ptr[nt % 4][:, :],
                                       sbw[i % 2][:, k * 128:(k + 1) * 128],
                                       sid[:, :]))
                    nt += 1

        @block.scalar
        def _(s):
            nt = 0
            for i, (wap, r0, IN, _, _, K) in enumerate(tiles):
                if i >= 2:
                    s.wait_ge(st[i % 2].h, 16 * ((i - 2) // 2 + 1))
                for k in range(K):
                    s.wait_ge(td.h, nt + 1)
                    ed.inc(s.copy(smcol[i % 2][:, k * 128:(k + 1) * 128],
                                  ptr[nt % 4][:, :]))
                    nt += 1
                fd.inc(s.drain())

        @block.sync
        def _(sync):
            for i, (wap, r0, IN, lap, off, K) in enumerate(tiles):
                sync.wait_ge(fd.h, i + 1)
                tot_free = lap.tensor.shape[1]
                dst = bass.AP(lap.tensor, off, [[tot_free, 128], [1, K * 128]])
                st[i % 2].inc(sync.dma_start(out=dst,
                                             in_=smcol[i % 2][:, 0:K * 128]),
                              16)
            sync.wait_ge(st[0].h, st[0].n)
            sync.wait_ge(st[1].h, st[1].n)

    P.close()
    return P.nc


def build_moe_launch():
    QC = 512
    NQ = TPC // QC
    P = Prog()
    nc = P.nc
    xin = nc.dram_tensor("xin", [D, TPC], F32, kind="ExternalInput").ap()
    gw = nc.dram_tensor("gw", [32, D], F32, kind="ExternalInput").ap()
    gbrow = nc.dram_tensor("gbrow", [1, 32], F32, kind="ExternalInput").ap()
    lwfc1 = nc.dram_tensor("lwfc1", [128, 8 * 16 * 128], BF16,
                           kind="ExternalInput").ap()
    lwfc2 = nc.dram_tensor("lwfc2", [128, 16 * 8 * 128], BF16,
                           kind="ExternalInput").ap()
    lwshu = nc.dram_tensor("lwshu", [128, 32 * 16 * 128], BF16,
                           kind="ExternalInput").ap()
    lwshd = nc.dram_tensor("lwshd", [128, 16 * 32 * 128], BF16,
                           kind="ExternalInput").ap()
    lw1 = nc.dram_tensor("lw1", [128, 256 * 8 * 128], BF16,
                         kind="ExternalInput").ap()
    lw2 = nc.dram_tensor("lw2", [128, 256 * 8 * 128], BF16,
                         kind="ExternalInput").ap()
    xout = nc.dram_tensor("xout", [D, TPC], F32, kind="ExternalOutput").ap()

    C = Ctx(P, QC)
    xnq = [P.sbuf(f"xnq{i}", [128, QC], BF16) for i in range(NCH)]
    mrowb = P.sbuf("mrowb", [1, TPC], BF16)
    rrowb = P.sbuf("rrowb", [1, TPC], BF16)
    gwT = P.sbuf("gwT", [128, 16 * 32], F32)
    sgwr = [P.sbuf(f"sgwr{i}", [32, 128], F32) for i in range(2)]
    gbb = P.sbuf("gbb", [128, 32], F32)
    ncsrow = P.sbuf("ncsrow", [1, 32], F32)
    idf = P.sbuf("idf", [128, 128], F32)
    rb32 = P.sbuf("rb32", [32, QC], BF16)
    lg32 = P.sbuf("lg32", [32, QC], F32)
    sct = P.sbuf("sct", [128, 16 * 32], F32)
    cwE = P.sbuf("cwE", [32, TPC], BF16)
    cwrow = P.sbuf("cwrow", [1, TPC], BF16)
    onesbr = P.sbuf("onesbr", [1, 128], BF16)
    cwb = [P.sbuf(f"cwb{i}", [128, QC], BF16) for i in range(NQ)]
    r0 = P.sbuf("r0", [128, 32], F32)
    r1 = P.sbuf("r1", [128, 32], F32)
    r2 = P.sbuf("r2", [128, 32], F32)
    r3 = P.sbuf("r3", [128, 32], F32)
    rg = P.sbuf("rg", [128, 16], F32)
    csb = P.sbuf("csb", [32, 1], F32)
    xlat = [P.sbuf(f"xlat{i}", [128, TPC], BF16) for i in range(8)]
    ylat = [P.sbuf(f"ylat{i}", [128, TPC], BF16) for i in range(8)]
    he = [P.sbuf(f"he{i}", [128, TPC], BF16) for i in range(8)]
    hsh = [P.sbuf(f"hsh{i}", [128, QC], BF16) for i in range(16)]
    zacc = [P.sbuf(f"zacc{i}", [128, QC], BF16) for i in range(16)]
    tmpf = [P.sbuf(f"tmpf{i}", [128, QC], BF16) for i in range(2)]
    wca = [P.sbuf(f"wca{i}", [128, 16 * 128], BF16) for i in range(2)]
    wcb = [P.sbuf(f"wcb{i}", [128, 8 * 128], BF16) for i in range(2)]
    _yt = P.sbuf("yt0", [128, QC], F32)
    yt = [_yt, _yt]
    _xr = P.sbuf("xr0", [128, QC], F32)
    xr = [_xr, _xr]
    ps = [P.psum(f"ps{i}", [128, QC], F32) for i in range(8)]

    lsm = P.sem("lsm")
    lga = [P.sem(f"lga{i}") for i in range(2)]
    lwa = [P.sem(f"lwa{i}") for i in range(2)]
    lwb = [P.sem(f"lwb{i}") for i in range(2)]
    ldr = [P.sem(f"ldr{i}") for i in range(2)]
    ldc = P.sem("ldc")
    stc = [P.sem(f"stc{i}") for i in range(2)]
    vch, sch, tch, gch = C.vch, C.sch, C.tch, C.gch

    C.emit_consts(P)

    with P.block("moeB0") as block:
        @block.gpsimd
        def _(g):
            gch.inc(g.memset(onesbr[:, :], 1.0))
            gch.inc(g.memset(idf[:, :], 1.0))
            g.wait_ge(gch.h, gch.n)
            gch.inc(g.affine_select(idf[:, :], idf[:, :], [[-1, 128]],
                                    ALU.is_equal, 0.0, base=0,
                                    channel_multiplier=1))
            g.wait_ge(gch.h, gch.n)

        @block.sync
        def _(sync):
            lsm.inc(sync.dma_start(out=gbb[0:1, :], in_=gbrow), 16)

    with P.block("moeB1") as block:
        T0, V0 = tch.n, vch.n
        @block.sync
        def _(sync):
            for k in range(NCH):
                if k >= 2:
                    sync.wait_ge(tch.h, T0 + k - 1)
                lga[k % 2].inc(sync.dma_start(
                    out=sgwr[k % 2][:, :],
                    in_=gw[:, k * 128:(k + 1) * 128]), 16)

        @block.tensor
        def _(t):
            t.wait_ge(lsm.h, 16)
            for k in range(NCH):
                t.wait_ge(lga[k % 2].h, 16 * (k // 2 + 1))
                if k >= 2:
                    t.wait_ge(vch.h, V0 + k - 1)
                tch.inc(t.transpose(ps[k % 2][:, 0:32],
                                    sgwr[k % 2][:, :],
                                    idf[0:32, 0:32]))
            tch.inc(t.matmul(ps[2][:, 0:32], C.onesr[:, :], gbb[0:1, 0:32],
                             start=True, stop=True))
            t.wait_ge(vch.h, V0 + 16)
            for k in range(NCH):
                mm = t.matmul(ps[3][0:32, 0:1], gwT[:, k * 32:(k + 1) * 32],
                              C.onesf[:, :], start=(k == 0),
                              stop=(k == NCH - 1))
            tch.inc(mm)
            t.wait_ge(vch.h, V0 + 18)
            tch.inc(t.transpose(ps[4][0:1, 0:32], csb[:, :],
                                idf[0:32, 0:32]))

        @block.vector
        def _(v):
            for k in range(NCH):
                v.wait_ge(tch.h, T0 + k + 1)
                vch.inc(v.tensor_copy(gwT[:, k * 32:(k + 1) * 32],
                                      ps[k % 2][:, 0:32]))
            v.wait_ge(tch.h, T0 + 17)
            vch.inc(v.tensor_copy(gbb[:, :], ps[2][:, 0:32]))
            v.wait_ge(tch.h, T0 + 18)
            vch.inc(v.tensor_copy(csb[:, :], ps[3][0:32, 0:1]))
            v.wait_ge(tch.h, T0 + 19)
            vch.inc(v.tensor_scalar_mul(ncsrow[0:1, :], ps[4][0:1, 0:32],
                                        -1.0))

    emit_ln(P, C, xin, None, ps, save_m=mrowb, save_r=rrowb)

    with P.block("moeB2") as block:
        T0, S0, V0 = tch.n, sch.n, vch.n
        LD0 = [C.ldn[0].n, C.ldn[1].n]
        rows_done, lg_done, evac_done = {}, {}, {}
        _vc = V0
        for _qf in range(NQ):
            if _qf > 0:
                _vc += 4
                evac_done[_qf - 1] = _vc
            _vc += 2
            rows_done[_qf] = _vc
            _vc += 2
            lg_done[_qf] = _vc
        _vc += 4
        evac_done[NQ - 1] = _vc
        def t_logit(qf, ch):
            return T0 + 22 * qf + ch + 1
        def t_rank1(qf):
            return T0 + 22 * qf + 17
        def t_bc(qf):
            return T0 + 22 * qf + 18
        def t_tr(qf, tb):
            return T0 + 22 * qf + 19 + tb
        sig_done = {qf: S0 + qf + 1 for qf in range(NQ)}

        @block.sync
        def _(sync):
            for qf in range(NQ):
                for ch in range(NCH):
                    i = qf * NCH + ch
                    if i >= 2:
                        i2 = i - 2
                        sync.wait_ge(tch.h, t_logit(i2 // NCH, i2 % NCH))
                    C.ldn[ch % 2].inc(sync.dma_start(
                        out=C.xq2[ch % 2][:, :],
                        in_=bass.AP(xin.tensor, (ch * 128) * TPC + qf * QC,
                                    [[TPC, 128], [1, QC]])), 16)

        @block.vector
        def _(v):
            def evacs(q):
                for tb in range(4):
                    ch = q * 4 + tb
                    v.wait_ge(tch.h, t_tr(q, tb))
                    vch.inc(v.tensor_copy(sct[:, ch * 32:(ch + 1) * 32],
                                          ps[4 + tb % 2][:, 0:32]))
            for qf in range(NQ):
                if qf > 0:
                    evacs(qf - 1)
                qsl = slice(qf * QC, (qf + 1) * QC)
                vch.inc(v.tensor_copy(C.hr0[0:1, :], mrowb[0:1, qsl]))
                vch.inc(v.tensor_copy(C.hr1[0:1, :], rrowb[0:1, qsl]))
                v.wait_ge(tch.h, t_bc(qf))
                vch.inc(v.tensor_copy(rb32[:, :], ps[1][0:32, :]))
                v.wait_ge(vch.h, vch.n)
                if qf > 0:
                    v.wait_ge(tch.h, t_tr(qf - 1, 3))
                vch.inc(v.tensor_tensor(lg32[:, :], ps[0][0:32, :],
                                        rb32[:, :], ALU.mult))
            evacs(NQ - 1)

        @block.scalar
        def _(s):
            for qf in range(NQ):
                s.wait_ge(vch.h, lg_done[qf])
                sch.inc(s.activation(lg32[:, :], lg32[:, :], AF.Sigmoid))

        @block.tensor
        def _(t):
            for qf in range(NQ):
                t.wait_ge(vch.h, rows_done[qf])
                if qf >= 1:
                    t.wait_ge(vch.h, lg_done[qf - 1])
                for ch in range(NCH):
                    i = qf * NCH + ch
                    t.wait_ge(C.ldn[ch % 2].h,
                              LD0[ch % 2] + 16 * (qf * 8 + ch // 2 + 1))
                    tch.inc(t.matmul(ps[0][0:32, :],
                                     gwT[:, ch * 32:(ch + 1) * 32],
                                     C.xq2[ch % 2][:, :],
                                     start=(ch == 0), stop=False))
                tch.inc(t.matmul(ps[0][0:32, :], ncsrow[0:1, :],
                                 C.hr0[0:1, :], start=False, stop=True))
                tch.inc(t.matmul(ps[1][0:32, :], onesbr[0:1, 0:32],
                                 rrowb[0:1, qf * QC:(qf + 1) * QC],
                                 start=True, stop=True))
                t.wait_ge(sch.h, sig_done[qf])
                if qf >= 1:
                    t.wait_ge(vch.h, evac_done[qf - 1])
                for tb in range(4):
                    if tb >= 2:
                        t.wait_ge(vch.h, evac_done[qf] - 3 + (tb - 2))
                    tch.inc(t.transpose(ps[4 + tb % 2][:, 0:32],
                                        lg32[:, tb * 128:(tb + 1) * 128],
                                        idf[0:32, 0:32]))

    with P.block("moeB3") as block:
        V0, T0 = vch.n, tch.n
        cw_done = {}

        @block.vector
        def _(v):
            def op(f):
                v.wait_ge(vch.h, vch.n)
                vch.inc(f())
            for ch in range(NCH):
                S = sct[:, ch * 32:(ch + 1) * 32]
                op(lambda: v.tensor_tensor(r0[:, :], S, gbb[:, :], ALU.add))
                for g in range(4):
                    sl = slice(8 * g, 8 * g + 8)
                    op(lambda: v.reduce_max(rg[:, g:g + 1], r0[:, sl], AX.X))
                    op(lambda: v.tensor_scalar(r2[:, sl], r0[:, sl],
                                       rg[:, g:g + 1], None, ALU.is_equal))
                    op(lambda: v.scalar_tensor_tensor(r2[:, sl], r2[:, sl], -1e30,
                                              r0[:, sl], ALU.mult, ALU.add))
                    op(lambda: v.reduce_max(rg[:, 4 + g:5 + g], r2[:, sl], AX.X))
                op(lambda: v.tensor_tensor(rg[:, 0:4], rg[:, 0:4], rg[:, 4:8],
                                   ALU.add))
                op(lambda: v.reduce_max(rg[:, 8:9], rg[:, 0:4], AX.X))
                op(lambda: v.tensor_scalar(rg[:, 12:16], rg[:, 0:4], rg[:, 8:9],
                                   None, ALU.is_equal))
                op(lambda: v.scalar_tensor_tensor(rg[:, 12:16], rg[:, 12:16], -1e30,
                                          rg[:, 0:4], ALU.mult, ALU.add))
                op(lambda: v.reduce_max(rg[:, 9:10], rg[:, 12:16], AX.X))
                op(lambda: v.tensor_scalar(rg[:, 12:16], rg[:, 0:4], rg[:, 9:10],
                                   None, ALU.is_ge))
                op(lambda: v.tensor_scalar(rg[:, 12:16], rg[:, 12:16], 1.0, 1e30,
                                   ALU.subtract, ALU.mult))
                for g in range(4):
                    sl = slice(8 * g, 8 * g + 8)
                    op(lambda: v.tensor_scalar(r1[:, sl], r0[:, sl],
                                       rg[:, 12 + g:13 + g], None, ALU.add))
                op(lambda: v.tensor_copy(r2[:, :], r1[:, :]))
                for it in range(7):
                    op(lambda: v.reduce_max(rg[:, 10:11], r1[:, :], AX.X))
                    op(lambda: v.tensor_scalar(r3[:, :], r1[:, :], rg[:, 10:11],
                                       None, ALU.is_equal))
                    op(lambda: v.scalar_tensor_tensor(r1[:, :], r3[:, :], -1e30,
                                              r1[:, :], ALU.mult, ALU.add))
                op(lambda: v.reduce_max(rg[:, 11:12], r1[:, :], AX.X))
                op(lambda: v.tensor_scalar(r3[:, :], r2[:, :], rg[:, 11:12], None,
                                   ALU.is_ge))
                op(lambda: v.tensor_tensor(r3[:, :], r3[:, :], S, ALU.mult))
                op(lambda: v.tensor_reduce(rg[:, 10:11], r3[:, :], AX.X, ALU.add))
                op(lambda: v.reciprocal(rg[:, 11:12], rg[:, 10:11]))
                op(lambda: v.tensor_scalar(sct[:, ch * 32:(ch + 1) * 32], r3[:, :],
                                   rg[:, 11:12], 2.5, ALU.mult, ALU.mult))
                cw_done[ch] = vch.n
            for ch in range(NCH):
                v.wait_ge(tch.h, T0 + ch + 1)
                vch.inc(v.tensor_copy(cwE[:, ch * 128:(ch + 1) * 128],
                                      ps[ch % 2][0:32, 0:128]))

        @block.tensor
        def _(t):
            for ch in range(NCH):
                t.wait_ge(vch.h, cw_done[ch])
                if ch >= 2:
                    t.wait_ge(vch.h, V0 + 56 * NCH + ch - 1)
                tch.inc(t.transpose(ps[ch % 2][0:32, 0:128],
                                    sct[:, ch * 32:(ch + 1) * 32],
                                    idf[:, :]))

    for qf in range(NQ):
        emit_xnq(P, C, xin, xnq, qf, mrowb, rrowb, ps)
        with P.block("moeB4") as block:
            T0, S0 = tch.n, sch.n
            LWA0 = [lwa[0].n, lwa[1].n]

            @block.sync
            def _(sync):
                for m in range(8):
                    if m >= 2:
                        sync.wait_ge(tch.h, T0 + m - 1)
                    lwa[m % 2].inc(sync.dma_start(
                        out=wca[m % 2][:, :],
                        in_=bass.AP(lwfc1.tensor, m * (16 * 128),
                                    [[8 * 16 * 128, 128], [1, 16 * 128]])),
                        16)

            @block.tensor
            def _(t):
                for m in range(8):
                    t.wait_ge(lwa[m % 2].h, LWA0[m % 2] + 16 * (m // 2 + 1))
                    if m >= 2:
                        t.wait_ge(sch.h, S0 + m - 1)
                    for k in range(NCH):
                        mm = t.matmul(ps[m % 2][:, :],
                                      wca[m % 2][:, k * 128:(k + 1) * 128],
                                      xnq[k][:, :],
                                      start=(k == 0), stop=(k == NCH - 1))
                    tch.inc(mm)

            @block.scalar
            def _(s):
                for m in range(8):
                    s.wait_ge(tch.h, T0 + m + 1)
                    sch.inc(s.copy(xlat[m][:, qf * QC:(qf + 1) * QC],
                                   ps[m % 2][:, :]))

    with P.block("moeB5") as block:
        @block.gpsimd
        def _(g):
            for m in range(8):
                gch.inc(g.memset(ylat[m][:, :], 0.0))
            g.wait_ge(gch.h, gch.n)

    for e in range(32):
        with P.block("moeB6") as block:
            T0, S0, V0 = tch.n, sch.n, vch.n
            LW0 = [lwb[0].n, lwb[1].n]
            he_done = {}
            tc = T0 + 4
            for mi in range(8):
                for qf in range(NQ):
                    tc += 1
                    he_done[(mi, qf)] = tc
            y_done = {}
            for mo in range(8):
                for qf in range(NQ):
                    tc += 1
                    y_done[(mo, qf)] = tc
            s_sq = {}
            sc = S0
            for mi in range(8):
                for qf in range(NQ):
                    sc += 2
                    s_sq[(mi, qf)] = sc
            v_cw = {qf: V0 + qf + 1 for qf in range(NQ)}
            v_he = {}
            vc = V0 + NQ
            for mi in range(8):
                for qf in range(NQ):
                    vc += 1
                    v_he[(mi, qf)] = vc
            v_y = {}
            for mo in range(8):
                for qf in range(NQ):
                    vc += 1
                    v_y[(mo, qf)] = vc

            @block.gpsimd
            def _(g):
                ldc.inc(g.dma_start(out=cwrow[0:1, :],
                                    in_=cwE[e:e + 1, :]), 16)

            @block.tensor
            def _(t):
                t.wait_ge(ldc.h, ldc.n)
                for qf in range(NQ):
                    if e > 0:
                        t.wait_ge(vch.h, V0)
                    tch.inc(t.matmul(ps[6][:, :], onesbr[:, :],
                                     cwrow[0:1, qf * QC:(qf + 1) * QC],
                                     start=True, stop=True))
                    t.wait_ge(vch.h, v_cw[qf])
                for mi in range(8):
                    t.wait_ge(lwb[mi % 2].h, LW0[mi % 2] + 16 * (mi // 2 + 1))
                    for qf in range(NQ):
                        i = mi * NQ + qf
                        if i >= 2:
                            i2 = i - 2
                            t.wait_ge(sch.h, s_sq[(i2 // NQ, i2 % NQ)] - 1)
                        for k in range(8):
                            mm = t.matmul(
                                ps[i % 2][:, :],
                                wcb[mi % 2][:, k * 128:(k + 1) * 128],
                                xlat[k][:, qf * QC:(qf + 1) * QC],
                                start=(k == 0), stop=(k == 7))
                        tch.inc(mm)
                for mo in range(8):
                    t.wait_ge(lwb[mo % 2].h,
                              LW0[mo % 2] + 16 * ((8 + mo) // 2 + 1))
                    for qf in range(NQ):
                        i = mo * NQ + qf
                        if i >= 2:
                            i2 = i - 2
                            t.wait_ge(vch.h, v_y[(i2 // NQ, i2 % NQ)])
                        t.wait_ge(vch.h, v_he[(7, qf)])
                        for k in range(8):
                            mm = t.matmul(
                                ps[2 + i % 2][:, :],
                                wcb[mo % 2][:, k * 128:(k + 1) * 128],
                                he[k][:, qf * QC:(qf + 1) * QC],
                                start=(k == 0), stop=(k == 7))
                        tch.inc(mm)

            @block.scalar
            def _(s):
                for mi in range(8):
                    for qf in range(NQ):
                        i = mi * NQ + qf
                        s.wait_ge(tch.h, he_done[(mi, qf)])
                        if i >= 2:
                            i2 = i - 2
                            s.wait_ge(vch.h, v_he[(i2 // NQ, i2 % NQ)])
                        sch.inc(s.activation(tmpf[i % 2][:, :],
                                             ps[i % 2][:, :], AF.Relu))
                        s.wait_ge(sch.h, sch.n)
                        sch.inc(s.square(tmpf[i % 2][:, :],
                                         tmpf[i % 2][:, :]))

            @block.vector
            def _(v):
                for qf in range(NQ):
                    v.wait_ge(tch.h, T0 + qf + 1)
                    vch.inc(v.tensor_copy(cwb[qf][:, :], ps[6][:, :]))
                for mi in range(8):
                    for qf in range(NQ):
                        i = mi * NQ + qf
                        v.wait_ge(sch.h, s_sq[(mi, qf)])
                        vch.inc(v.tensor_tensor(
                            he[mi][:, qf * QC:(qf + 1) * QC],
                            tmpf[i % 2][:, :], cwb[qf][:, :], ALU.mult))
                for mo in range(8):
                    for qf in range(NQ):
                        v.wait_ge(tch.h, y_done[(mo, qf)])
                        vch.inc(v.tensor_tensor(
                            ylat[mo][:, qf * QC:(qf + 1) * QC],
                            ylat[mo][:, qf * QC:(qf + 1) * QC],
                            ps[2 + (mo * NQ + qf) % 2][:, :], ALU.add))

            @block.sync
            def _(sync):
                for mi in range(8):
                    if mi >= 2:
                        sync.wait_ge(tch.h, he_done[(mi - 2, NQ - 1)])
                    lwb[mi % 2].inc(sync.dma_start(
                        out=wcb[mi % 2][:, 0:8 * 128],
                        in_=bass.AP(lw1.tensor, (e * 8 + mi) * (8 * 128),
                                    [[256 * 8 * 128, 128], [1, 8 * 128]])),
                        16)
                for mo in range(8):
                    sync.wait_ge(tch.h, he_done[(mo + 6, NQ - 1)]
                                 if mo + 6 < 8 else he_done[(7, NQ - 1)])
                    if mo >= 2:
                        sync.wait_ge(tch.h, y_done[(mo - 2, NQ - 1)])
                    lwb[mo % 2].inc(sync.dma_start(
                        out=wcb[mo % 2][:, 0:8 * 128],
                        in_=bass.AP(lw2.tensor, (e * 8 + mo) * (8 * 128),
                                    [[256 * 8 * 128, 128], [1, 8 * 128]])),
                        16)

    for qf in range(NQ):
        qsl = slice(qf * QC, (qf + 1) * QC)
        emit_xnq(P, C, xin, xnq, qf, mrowb, rrowb, ps)
        with P.block("moeB7") as block:
            T0, S0, V0 = tch.n, sch.n, vch.n
            LWA0 = [lwa[0].n, lwa[1].n]
            LWB0 = [lwb[0].n, lwb[1].n]
            LR0 = [ldr[0].n, ldr[1].n]
            ST0 = [stc[0].n, stc[1].n]
            fc2_done = {j: T0 + j + 1 for j in range(16)}
            shu_done = {}
            shd_done = {}
            tc = T0 + 16
            for half in range(2):
                for mu in range(16):
                    tc += 1
                    shu_done[(half, mu)] = tc
                for j in range(16):
                    tc += 1
                    shd_done[(half, j)] = tc
            sched = []
            for _h in range(2):
                for _mu in range(16):
                    sched.append(("shu", _h, _mu))
                for _j in range(16):
                    sched.append(("shd", _h, _j))
            pos_of = {(k, h, i): p for p, (k, h, i) in enumerate(sched)}
            ssq = {}
            sc = S0
            for half in range(2):
                for mu in range(16):
                    sc += 2
                    ssq[(half, mu)] = sc
            zc_done = {j: V0 + j + 1 for j in range(16)}
            za_done = {}
            vc = V0 + 16
            for half in range(2):
                for j in range(16):
                    vc += 1
                    za_done[(half, j)] = vc
            yt_done = {}
            for j in range(16):
                vc += 2
                yt_done[j] = vc

            @block.tensor
            def _(t):
                for j in range(16):
                    t.wait_ge(lwb[j % 2].h, LWB0[j % 2] + 16 * (j // 2 + 1))
                    if j >= 2:
                        t.wait_ge(vch.h, zc_done[j - 2])
                    for k in range(8):
                        mm = t.matmul(ps[j % 2][:, :],
                                      wcb[j % 2][:, k * 128:(k + 1) * 128],
                                      ylat[k][:, qsl],
                                      start=(k == 0), stop=(k == 7))
                    tch.inc(mm)
                for half in range(2):
                    for mu in range(16):
                        L = pos_of[("shu", half, mu)]
                        t.wait_ge(lwa[L % 2].h,
                                  LWA0[L % 2] + 16 * (L // 2 + 1))
                        if (half, mu) > (0, 1):
                            p2 = (half, mu - 2) if mu >= 2 \
                                else (half - 1, 14 + mu)
                            t.wait_ge(sch.h, ssq[p2] - 1)
                        for k in range(NCH):
                            mm = t.matmul(
                                ps[2 + mu % 2][:, :],
                                wca[L % 2][:, k * 128:(k + 1) * 128],
                                xnq[k][:, :],
                                start=(k == 0), stop=(k == NCH - 1))
                        tch.inc(mm)
                    for j in range(16):
                        L = pos_of[("shd", half, j)]
                        t.wait_ge(lwa[L % 2].h,
                                  LWA0[L % 2] + 16 * (L // 2 + 1))
                        t.wait_ge(sch.h, ssq[(half, 15)])
                        if j >= 2:
                            t.wait_ge(vch.h, za_done[(half, j - 2)])
                        elif half == 1:
                            t.wait_ge(vch.h, za_done[(0, 14 + j)])
                        for k in range(16):
                            mm = t.matmul(
                                ps[4 + j % 2][:, :],
                                wca[L % 2][:, k * 128:(k + 1) * 128],
                                hsh[k][:, :],
                                start=(k == 0), stop=(k == 15))
                        tch.inc(mm)

            @block.scalar
            def _(s):
                for half in range(2):
                    for mu in range(16):
                        s.wait_ge(tch.h, shu_done[(half, mu)])
                        sch.inc(s.activation(tmpf[mu % 2][:, :],
                                             ps[2 + mu % 2][:, :], AF.Relu))
                        if half == 1:
                            s.wait_ge(tch.h, shd_done[(0, 15)])
                        s.wait_ge(sch.h, sch.n)
                        sch.inc(s.square(hsh[mu][:, :], tmpf[mu % 2][:, :]))

            @block.vector
            def _(v):
                for j in range(16):
                    v.wait_ge(tch.h, fc2_done[j])
                    vch.inc(v.tensor_copy(zacc[j][:, :], ps[j % 2][:, :]))
                for half in range(2):
                    for j in range(16):
                        v.wait_ge(tch.h, shd_done[(half, j)])
                        vch.inc(v.tensor_tensor(zacc[j][:, :], zacc[j][:, :],
                                                ps[4 + j % 2][:, :],
                                                ALU.add))
                for j in range(16):
                    v.wait_ge(ldr[j % 2].h, LR0[j % 2] + 16 * (j // 2 + 1))
                    if j >= 1:
                        jp = j - 1
                        v.wait_ge(stc[jp % 2].h,
                                  ST0[jp % 2] + 16 * (jp // 2 + 1))
                    elif qf > 0:
                        v.wait_ge(stc[0].h, ST0[0])
                        v.wait_ge(stc[1].h, ST0[1])
                    v.wait_ge(vch.h, vch.n)
                    vch.inc(v.scalar_tensor_tensor(
                        yt[j % 2][:, :], zacc[j][:, :], 0.5, xr[j % 2][:, :],
                        ALU.mult, ALU.add))
                    vch.inc(v.drain())

            @block.sync
            def _(sync):
                for j in range(16):
                    if j >= 2:
                        sync.wait_ge(tch.h, fc2_done[j - 2])
                    lwb[j % 2].inc(sync.dma_start(
                        out=wcb[j % 2][:, 0:8 * 128],
                        in_=bass.AP(lwfc2.tensor, j * (8 * 128),
                                    [[16 * 8 * 128, 128], [1, 8 * 128]])),
                        16)
                for pos, (kind, half, idx) in enumerate(sched):
                    if pos >= 2:
                        k2, h2, i2 = sched[pos - 2]
                        done = shu_done[(h2, i2)] if k2 == "shu" \
                            else shd_done[(h2, i2)]
                        sync.wait_ge(tch.h, done)
                    if kind == "shu":
                        src = bass.AP(lwshu.tensor,
                                      (half * 16 + idx) * (16 * 128),
                                      [[32 * 16 * 128, 128], [1, 16 * 128]])
                    else:
                        src = bass.AP(lwshd.tensor,
                                      idx * (32 * 128) + half * (16 * 128),
                                      [[16 * 32 * 128, 128], [1, 16 * 128]])
                    lwa[pos % 2].inc(sync.dma_start(
                        out=wca[pos % 2][:, :], in_=src), 16)

            @block.gpsimd
            def _(g):
                for j in range(16):
                    if j >= 1:
                        g.wait_ge(vch.h, yt_done[j - 1] - 1)
                    ldr[j % 2].inc(g.dma_start(
                        out=xr[j % 2][:, :],
                        in_=bass.AP(xin.tensor, (j * 128) * TPC + qf * QC,
                                    [[TPC, 128], [1, QC]])), 16)
                    if j >= 1:
                        g.wait_ge(vch.h, yt_done[j - 1])
                        stc[(j - 1) % 2].inc(g.dma_start(
                            out=bass.AP(xout.tensor,
                                        ((j - 1) * 128) * TPC + qf * QC,
                                        [[TPC, 128], [1, QC]]),
                            in_=yt[(j - 1) % 2][:, :]), 16)
                g.wait_ge(vch.h, yt_done[15])
                stc[15 % 2].inc(g.dma_start(
                    out=bass.AP(xout.tensor, (15 * 128) * TPC + qf * QC,
                                [[TPC, 128], [1, QC]]),
                    in_=yt[15 % 2][:, :]), 16)

    with P.block("moeB8") as block:
        @block.gpsimd
        def _(g):
            g.wait_ge(stc[0].h, stc[0].n)
            g.wait_ge(stc[1].h, stc[1].n)
    P.close()
    return P.nc




def build_xy_launch():
    QC = 512
    NQ = TPC // QC
    P = Prog()
    nc = P.nc
    xin = nc.dram_tensor("xin", [D, TPC], F32, kind="ExternalInput").ap()
    lwxy = nc.dram_tensor("lwxy", [128, 32 * 16 * 128], BF16,
                          kind="ExternalInput").ap()
    xrT = nc.dram_tensor("xrT", [D, TPC], F32, kind="ExternalOutput").ap()
    yT = nc.dram_tensor("yT", [D, TPC], F32, kind="ExternalOutput").ap()

    C = Ctx(P, QC)
    xn = [P.sbuf(f"xn{i}", [128, TPC], BF16) for i in range(NCH)]
    wcol = [P.sbuf(f"wcol{i}", [128, 16 * 128], BF16) for i in range(2)]
    ot = [P.sbuf(f"ot{i}", [128, TPC], F32) for i in range(2)]
    ps = [P.psum(f"ps{i}", [128, QC], F32) for i in range(8)]
    lwl = [P.sem(f"lwl{i}") for i in range(2)]
    stc = [P.sem(f"stc{i}") for i in range(2)]
    vch, sch, tch = C.vch, C.sch, C.tch

    C.emit_consts(P)
    emit_ln(P, C, xin, xn, ps)

    with P.block("xyP") as block:
        T0, S0 = tch.n, sch.n
        LW0 = [lwl[0].n, lwl[1].n]
        ST0 = [stc[0].n, stc[1].n]
        mm_done = {(m, qf): T0 + m * NQ + qf + 1
                   for m in range(32) for qf in range(NQ)}
        ev_done = {(m, qf): S0 + m * (NQ + 1) + qf + 1
                   for m in range(32) for qf in range(NQ)}
        fl_done = {m: S0 + (m + 1) * (NQ + 1) for m in range(32)}

        @block.sync
        def _(sync):
            for m in range(32):
                if m >= 2:
                    sync.wait_ge(tch.h, mm_done[(m - 2, NQ - 1)])
                lwl[m % 2].inc(sync.dma_start(
                    out=wcol[m % 2][:, :],
                    in_=bass.AP(lwxy.tensor, m * (16 * 128),
                                [[32 * 16 * 128, 128], [1, 16 * 128]])), 16)
                sync.wait_ge(sch.h, fl_done[m])
                outap = xrT if m < 16 else yT
                r0 = (m % 16) * 128
                stc[m % 2].inc(sync.dma_start(
                    out=bass.AP(outap.tensor, r0 * TPC, [[TPC, 128], [1, TPC]]),
                    in_=ot[m % 2][:, :]), 16)

        @block.tensor
        def _(t):
            for m in range(32):
                t.wait_ge(lwl[m % 2].h, LW0[m % 2] + 16 * (m // 2 + 1))
                for qf in range(NQ):
                    i = m * NQ + qf
                    if i >= 2:
                        i2 = i - 2
                        t.wait_ge(sch.h, ev_done[(i2 // NQ, i2 % NQ)])
                    for k in range(NCH):
                        mm = t.matmul(ps[i % 2][:, :],
                                      wcol[m % 2][:, k * 128:(k + 1) * 128],
                                      xn[k][:, qf * QC:(qf + 1) * QC],
                                      start=(k == 0), stop=(k == NCH - 1))
                    tch.inc(mm)

        @block.scalar
        def _(s):
            for m in range(32):
                if m >= 2:
                    s.wait_ge(stc[m % 2].h, ST0[m % 2] + 16 * ((m - 2) // 2 + 1))
                for qf in range(NQ):
                    s.wait_ge(tch.h, mm_done[(m, qf)])
                    sch.inc(s.copy(ot[m % 2][:, qf * QC:(qf + 1) * QC],
                                   ps[(m * NQ + qf) % 2][:, :]))
                sch.inc(s.drain())

    with P.block("xyEnd") as block:
        @block.gpsimd
        def _(g):
            g.wait_ge(stc[0].h, stc[0].n)
            g.wait_ge(stc[1].h, stc[1].n)
    P.close()
    return P.nc


def build_scan_launch():
    TC = 512
    NT = TPC // TC
    P = Prog()
    nc = P.nc
    xrT = nc.dram_tensor("xrT", [D, TPC], F32, kind="ExternalInput").ap()
    halo = nc.dram_tensor("halo", [D, 3], F32, kind="ExternalInput").ap()
    cwc = nc.dram_tensor("cwc", [128, 64], F32, kind="ExternalInput").ap()
    cbc = nc.dram_tensor("cbc", [128, 16], F32, kind="ExternalInput").ap()
    igw = nc.dram_tensor("igw", [16, 128, 128], F32, kind="ExternalInput").ap()
    agw = nc.dram_tensor("agw", [16, 128, 128], F32, kind="ExternalInput").ap()
    igbc = nc.dram_tensor("igbc", [128, 16], F32, kind="ExternalInput").ap()
    agbc = nc.dram_tensor("agbc", [128, 16], F32, kind="ExternalInput").ap()
    apc = nc.dram_tensor("apc", [128, 16], F32, kind="ExternalInput").ap()
    hT = nc.dram_tensor("hT", [D, TPC], F32, kind="ExternalOutput").ap()
    acT = nc.dram_tensor("acT", [D, TPC], F32, kind="ExternalOutput").ap()

    xrp = [P.sbuf(f"xrp{i}", [128, TC + 3], F32) for i in range(2)]
    xc = [P.sbuf(f"xc{i}", [128, TC], F32) for i in range(NCH)]
    xcb = [P.sbuf(f"xcb{i}", [128, TC], BF16) for i in range(NCH)]
    gx = [P.sbuf(f"gx{i}", [128, TC], F32) for i in range(NCH)]
    aa = [P.sbuf(f"aa{i}", [128, TC], F32) for i in range(NCH)]
    mlt = [P.sbuf(f"mlt{i}", [128, TC], F32) for i in range(NCH)]
    hsc = [P.sbuf(f"hsc{i}", [128, TC], F32) for i in range(2)]
    asc = [P.sbuf(f"asc{i}", [128, TC], F32) for i in range(2)]
    sgw = [P.sbuf(f"sgw{i}", [128, 128], BF16) for i in range(32)]
    sgf = P.sbuf("sgf", [128, 128], F32)
    scw = P.sbuf("scw", [128, 64], F32)
    scb = P.sbuf("scb", [128, 16], F32)
    sib = P.sbuf("sib", [128, 16], F32)
    sab = P.sbuf("sab", [128, 16], F32)
    sap = P.sbuf("sap", [128, 16], F32)
    spm = P.sbuf("spm", [128, 16], F32)
    spt = P.sbuf("spt", [128, 16], F32)
    hcar = P.sbuf("hcar", [128, 16], F32)
    acar = P.sbuf("acar", [128, 16], F32)
    zz = P.sbuf("zz", [128, 512], F32)
    ps = [P.psum(f"ps{i}", [128, TC], F32) for i in range(8)]

    lsm = P.sem("lsm")
    lgg = P.sem("lgg")
    lxr = [P.sem(f"lxr{i}") for i in range(2)]
    vch = P.sem("vch")
    sch = P.sem("sch")
    tch = P.sem("tch")
    gch = P.sem("gch")
    stc = [P.sem(f"stc{i}") for i in range(2)]

    with P.block("scP0") as block:
        @block.sync
        def _(sync):
            lsm.inc(sync.dma_start(out=scw[:, :], in_=cwc), 16)
            lsm.inc(sync.dma_start(out=scb[:, :], in_=cbc), 16)
            lsm.inc(sync.dma_start(out=sib[:, :], in_=igbc), 16)
            lsm.inc(sync.dma_start(out=sab[:, :], in_=agbc), 16)
            lsm.inc(sync.dma_start(out=sap[:, :], in_=apc), 16)
            for n in range(16):
                lgg.inc(sync.dma_start(out=sgf[:, :],
                                       in_=igw[n, :, :]), 16)
                sync.wait_ge(vch.h, 2 * n + 1)
                lgg.inc(sync.dma_start(out=sgf[:, :],
                                       in_=agw[n, :, :]), 16)
                sync.wait_ge(vch.h, 2 * n + 2)

        @block.vector
        def _(v):
            for n in range(16):
                v.wait_ge(lgg.h, 16 * (1 + 2 * n))
                vch.inc(v.tensor_copy(sgw[2 * n][:, :], sgf[:, :]))
                v.wait_ge(lgg.h, 16 * (2 + 2 * n))
                vch.inc(v.tensor_copy(sgw[2 * n + 1][:, :], sgf[:, :]))

        @block.gpsimd
        def _(g):
            gch.inc(g.memset(hcar[:, :], 0.0))
            gch.inc(g.memset(acar[:, :], 1.0))
            gch.inc(g.memset(zz[:, :], 0.0))
            g.wait_ge(gch.h, gch.n)

    with P.block("scP1") as block:
        @block.vector
        def _(v):
            v.wait_ge(lsm.h, 16 * 5)
            vch.inc(v.tensor_scalar_mul(spt[:, :], sap[:, :], -1.0))
            v.wait_ge(vch.h, vch.n)
            vch.inc(v.tensor_tensor(spt[:, :], spt[:, :], sap[:, :],
                                    ALU.min))

        @block.scalar
        def _(s):
            s.wait_ge(vch.h, vch.n)
            sch.inc(s.activation(spt[:, :], spt[:, :], AF.Exp))
            s.wait_ge(sch.h, sch.n)
            sch.inc(s.activation(spt[:, :], spt[:, :], AF.Ln, bias=1.0))

        @block.vector
        def _(v):
            v.wait_ge(sch.h, sch.n)
            vch.inc(v.tensor_single_scalar(spm[:, :], sap[:, :], 0.0,
                                           ALU.max))
            v.wait_ge(vch.h, vch.n)
            vch.inc(v.tensor_tensor(spm[:, :], spm[:, :], spt[:, :],
                                    ALU.add))
            v.wait_ge(vch.h, vch.n)
            vch.inc(v.tensor_scalar_mul(spm[:, :], spm[:, :], -8.0))

    for tc in range(NT):
        with P.block("scT") as block:
            T0, S0, V0 = tch.n, sch.n, vch.n
            LX0 = [lxr[0].n, lxr[1].n]
            ST0 = [stc[0].n, stc[1].n]
            conv_done = {ch: V0 + 4 * (ch + 1) for ch in range(NCH)}
            VA = V0 + 4 * NCH

            @block.sync
            def _(sync):
                for ch in range(NCH):
                    if ch >= 2:
                        sync.wait_ge(vch.h, conv_done[ch - 2])
                    if tc == 0:
                        lxr[ch % 2].inc(sync.dma_start(
                            out=xrp[ch % 2][:, 0:3],
                            in_=halo[ch * 128:(ch + 1) * 128, :]), 16)
                        lxr[ch % 2].inc(sync.dma_start(
                            out=xrp[ch % 2][:, 3:],
                            in_=bass.AP(xrT.tensor, (ch * 128) * TPC,
                                        [[TPC, 128], [1, TC]])), 16)
                    else:
                        lxr[ch % 2].inc(sync.dma_start(
                            out=xrp[ch % 2][:, :],
                            in_=bass.AP(xrT.tensor,
                                        (ch * 128) * TPC + tc * TC - 3,
                                        [[TPC, 128], [1, TC + 3]])), 16)

            @block.vector
            def _(v):
                KL = 32 if tc == 0 else 16
                for ch in range(NCH):
                    v.wait_ge(lxr[ch % 2].h, LX0[ch % 2] + KL * (ch // 2 + 1))
                    vch.inc(v.tensor_scalar(
                        xc[ch][:, :], xrp[ch % 2][:, 3:3 + TC],
                        scw[:, 4 * ch + 3:4 * ch + 4], scb[:, ch:ch + 1],
                        ALU.mult, ALU.add))
                    for i in range(3):
                        v.wait_ge(vch.h, vch.n)
                        vch.inc(v.scalar_tensor_tensor(
                            xc[ch][:, :], xrp[ch % 2][:, i:i + TC],
                            scw[:, 4 * ch + i:4 * ch + i + 1], xc[ch][:, :],
                            ALU.mult, ALU.add))

            @block.scalar
            def _(s):
                for ch in range(NCH):
                    s.wait_ge(vch.h, conv_done[ch])
                    sch.inc(s.copy(xcb[ch][:, :], xc[ch][:, :]))

            @block.tensor
            def _(t):
                for ch in range(NCH):
                    t.wait_ge(sch.h, S0 + ch + 1)
                    if ch >= 1:
                        t.wait_ge(sch.h, S0 + NCH + 2 * (ch - 1) + 2)
                    tch.inc(t.matmul(ps[2 * (ch % 2)][:, :],
                                     sgw[2 * ch][:, :], xcb[ch][:, :],
                                     start=True, stop=True))
                    tch.inc(t.matmul(ps[2 * (ch % 2) + 1][:, :],
                                     sgw[2 * ch + 1][:, :], xcb[ch][:, :],
                                     start=True, stop=True))

            @block.scalar
            def _(s):
                for ch in range(NCH):
                    s.wait_ge(tch.h, T0 + 2 * ch + 1)
                    sch.inc(s.activation(gx[ch][:, :],
                                         ps[2 * (ch % 2)][:, :], AF.Sigmoid,
                                         bias=sib[:, ch:ch + 1]))
                    s.wait_ge(tch.h, T0 + 2 * ch + 2)
                    sch.inc(s.activation(aa[ch][:, :],
                                         ps[2 * (ch % 2) + 1][:, :],
                                         AF.Sigmoid, bias=sab[:, ch:ch + 1]))

            @block.vector
            def _(v):
                SG0 = S0 + NCH + 2 * NCH
                for ch in range(NCH):
                    v.wait_ge(sch.h, S0 + NCH + 2 * (ch + 1))
                    vch.inc(v.tensor_scalar(aa[ch][:, :], aa[ch][:, :],
                                            spm[:, ch:ch + 1], None,
                                            ALU.mult))

            @block.scalar
            def _(s):
                V1 = VA + NCH
                for ch in range(NCH):
                    s.wait_ge(vch.h, VA + ch + 1)
                    sch.inc(s.activation(aa[ch][:, :], aa[ch][:, :], AF.Exp))
                    s.wait_ge(sch.h, sch.n)
                    sch.inc(s.square(mlt[ch][:, :], aa[ch][:, :]))

            @block.vector
            def _(v):
                S1 = S0 + 3 * NCH
                for ch in range(NCH):
                    v.wait_ge(sch.h, S1 + 2 * (ch + 1))
                    vch.inc(v.tensor_scalar(mlt[ch][:, :], mlt[ch][:, :],
                                            -1.0, 1.0, ALU.mult, ALU.add))

            @block.scalar
            def _(s):
                for ch in range(NCH):
                    s.wait_ge(vch.h, VA + NCH + ch + 1)
                    sch.inc(s.sqrt(mlt[ch][:, :], mlt[ch][:, :]))

            @block.vector
            def _(v):
                S2 = S0 + 5 * NCH
                V3 = VA + 2 * NCH
                for ch in range(NCH):
                    v.wait_ge(sch.h, S2 + ch + 1)
                    vch.inc(v.tensor_tensor(mlt[ch][:, :], gx[ch][:, :],
                                            mlt[ch][:, :], ALU.mult))
                    v.wait_ge(vch.h, vch.n)
                    vch.inc(v.tensor_tensor(xc[ch][:, :], xc[ch][:, :],
                                            mlt[ch][:, :], ALU.mult))
                    v.wait_ge(vch.h, vch.n)
                    if ch >= 2:
                        v.wait_ge(stc[ch % 2].h,
                                  ST0[ch % 2] + 32 * ((ch - 2) // 2 + 1))
                    elif tc > 0:
                        v.wait_ge(stc[ch % 2].h, ST0[ch % 2])
                    vch.inc(v.tensor_tensor_scan(
                        hsc[ch % 2][:, :], aa[ch][:, :], xc[ch][:, :],
                        hcar[:, ch:ch + 1], ALU.mult, ALU.add))
                    v.wait_ge(vch.h, vch.n)
                    vch.inc(v.tensor_copy(hcar[:, ch:ch + 1],
                                          hsc[ch % 2][:, TC - 1:TC]))
                    v.wait_ge(vch.h, vch.n)
                    vch.inc(v.tensor_tensor_scan(
                        asc[ch % 2][:, :], aa[ch][:, :], zz[:, :],
                        acar[:, ch:ch + 1], ALU.mult, ALU.add))
                    v.wait_ge(vch.h, vch.n)
                    vch.inc(v.tensor_copy(acar[:, ch:ch + 1],
                                          asc[ch % 2][:, TC - 1:TC]))
                    vch.inc(v.drain())

            @block.gpsimd
            def _(g):
                V3 = VA + 2 * NCH
                for ch in range(NCH):
                    g.wait_ge(vch.h, V3 + 7 * ch + 7)
                    stc[ch % 2].inc(g.dma_start(
                        out=bass.AP(hT.tensor, (ch * 128) * TPC + tc * TC,
                                    [[TPC, 128], [1, TC]]),
                        in_=hsc[ch % 2][:, :]), 16)
                    stc[ch % 2].inc(g.dma_start(
                        out=bass.AP(acT.tensor, (ch * 128) * TPC + tc * TC,
                                    [[TPC, 128], [1, TC]]),
                        in_=asc[ch % 2][:, :]), 16)

    with P.block("scEnd") as block:
        @block.gpsimd
        def _(g):
            g.wait_ge(stc[0].h, stc[0].n)
            g.wait_ge(stc[1].h, stc[1].n)
    P.close()
    return P.nc


def build_gout_launch():
    QC = 512
    NQ = TPC // QC
    P = Prog()
    nc = P.nc
    hT = nc.dram_tensor("hT", [D, TPC], F32, kind="ExternalInput").ap()
    acT = nc.dram_tensor("acT", [D, TPC], F32, kind="ExternalInput").ap()
    yT = nc.dram_tensor("yT", [D, TPC], F32, kind="ExternalInput").ap()
    carryc = nc.dram_tensor("carryc", [128, 16], F32,
                            kind="ExternalInput").ap()
    lwres = nc.dram_tensor("lwres", [128, 16 * 16 * 128], BF16,
                           kind="ExternalInput").ap()
    xin = nc.dram_tensor("xin", [D, TPC], F32, kind="ExternalInput").ap()
    xout = nc.dram_tensor("xout", [D, TPC], F32, kind="ExternalOutput").ap()

    ghb = [P.sbuf(f"ghb{i}", [128, QC], BF16) for i in range(NCH)]
    hh = [P.sbuf(f"hh{i}", [128, QC], F32) for i in range(2)]
    ac = [P.sbuf(f"ac{i}", [128, QC], F32) for i in range(2)]
    yy = [P.sbuf(f"yy{i}", [128, QC], F32) for i in range(2)]
    scar = P.sbuf("scar", [128, 16], F32)
    wcol = [P.sbuf(f"wcol{i}", [128, 16 * 128], BF16) for i in range(2)]
    yt = [P.sbuf(f"yt{i}", [128, QC], F32) for i in range(2)]
    xr = [P.sbuf(f"xr{i}", [128, QC], F32) for i in range(2)]
    ps = [P.psum(f"ps{i}", [128, QC], F32) for i in range(8)]

    lsm = P.sem("lsm")
    lha = [P.sem(f"lha{i}") for i in range(2)]
    lwl = [P.sem(f"lwl{i}") for i in range(2)]
    ldr = [P.sem(f"ldr{i}") for i in range(2)]
    vch = P.sem("vch")
    sch = P.sem("sch")
    tch = P.sem("tch")
    stc = [P.sem(f"stc{i}") for i in range(2)]

    with P.block("goP0") as block:
        @block.sync
        def _(sync):
            lsm.inc(sync.dma_start(out=scar[:, :], in_=carryc), 16)
            sync.wait_ge(lsm.h, 16)

    for qc in range(NQ):
        qsl = slice(qc * QC, (qc + 1) * QC)
        with P.block("goQ") as block:
            T0, S0, V0 = tch.n, sch.n, vch.n
            LH0 = [lha[0].n, lha[1].n]
            LW0 = [lwl[0].n, lwl[1].n]
            LR0 = [ldr[0].n, ldr[1].n]
            ST0 = [stc[0].n, stc[1].n]
            fix_done = {ch: V0 + 2 * ch + 1 for ch in range(NCH)}
            gh_done = {ch: V0 + 2 * ch + 2 for ch in range(NCH)}
            gel_done = {ch: S0 + ch + 1 for ch in range(NCH)}
            VB = V0 + 2 * NCH
            mm_done = {(j, qf4): T0 + j + 1 for j in range(16)
                       for qf4 in (0,)}
            ytm = {j: VB + 3 * j + 2 for j in range(16)}
            ytf = {j: VB + 3 * j + 3 for j in range(16)}

            @block.sync
            def _(sync):
                for ch in range(NCH):
                    if ch >= 2:
                        sync.wait_ge(vch.h, gh_done[ch - 2])
                    lha[ch % 2].inc(sync.dma_start(
                        out=hh[ch % 2][:, :],
                        in_=bass.AP(hT.tensor, (ch * 128) * TPC + qc * QC,
                                    [[TPC, 128], [1, QC]])), 16)
                    lha[ch % 2].inc(sync.dma_start(
                        out=ac[ch % 2][:, :],
                        in_=bass.AP(acT.tensor, (ch * 128) * TPC + qc * QC,
                                    [[TPC, 128], [1, QC]])), 16)
                    lha[ch % 2].inc(sync.dma_start(
                        out=yy[ch % 2][:, :],
                        in_=bass.AP(yT.tensor, (ch * 128) * TPC + qc * QC,
                                    [[TPC, 128], [1, QC]])), 16)
                for j in range(16):
                    if j >= 2:
                        sync.wait_ge(tch.h, T0 + j - 1)
                    lwl[j % 2].inc(sync.dma_start(
                        out=wcol[j % 2][:, :],
                        in_=bass.AP(lwres.tensor, j * (16 * 128),
                                    [[16 * 16 * 128, 128], [1, 16 * 128]])),
                        16)

            @block.scalar
            def _(s):
                for ch in range(NCH):
                    s.wait_ge(lha[ch % 2].h, LH0[ch % 2] + 48 * (ch // 2 + 1))
                    if ch >= 2:
                        s.wait_ge(vch.h, gh_done[ch - 2])
                    sch.inc(s.activation(yy[ch % 2][:, :], yy[ch % 2][:, :],
                                         AF.Gelu))

            @block.vector
            def _(v):
                for ch in range(NCH):
                    v.wait_ge(lha[ch % 2].h, LH0[ch % 2] + 48 * (ch // 2 + 1))
                    vch.inc(v.scalar_tensor_tensor(
                        ac[ch % 2][:, :], ac[ch % 2][:, :],
                        scar[:, ch:ch + 1], hh[ch % 2][:, :],
                        ALU.mult, ALU.add))
                    v.wait_ge(vch.h, vch.n)
                    v.wait_ge(sch.h, gel_done[ch])
                    vch.inc(v.tensor_tensor(ghb[ch][:, :], yy[ch % 2][:, :],
                                            ac[ch % 2][:, :], ALU.mult))
                for j in range(16):
                    v.wait_ge(tch.h, T0 + j + 1)
                    v.wait_ge(ldr[j % 2].h, LR0[j % 2] + 16 * (j // 2 + 1))
                    if j >= 2:
                        v.wait_ge(stc[j % 2].h,
                                  ST0[j % 2] + 16 * ((j - 2) // 2 + 1))
                    elif qc > 0:
                        v.wait_ge(stc[j % 2].h, ST0[j % 2])
                    vch.inc(v.tensor_tensor(yt[j % 2][:, :],
                                            ps[j % 2][:, :],
                                            xr[j % 2][:, :], ALU.add))
                    vch.inc(v.engine_nop())
                    vch.inc(v.drain())

            @block.tensor
            def _(t):
                for j in range(16):
                    t.wait_ge(lwl[j % 2].h, LW0[j % 2] + 16 * (j // 2 + 1))
                    if j >= 2:
                        t.wait_ge(vch.h, ytm[j - 2])
                    for k in range(NCH):
                        t.wait_ge(vch.h, gh_done[k]) if j == 0 else None
                        mm = t.matmul(ps[j % 2][:, :],
                                      wcol[j % 2][:, k * 128:(k + 1) * 128],
                                      ghb[k][:, :],
                                      start=(k == 0), stop=(k == NCH - 1))
                    tch.inc(mm)

            @block.gpsimd
            def _(g):
                for j in range(16):
                    if j >= 1:
                        g.wait_ge(vch.h, ytm[j - 1])
                    ldr[j % 2].inc(g.dma_start(
                        out=xr[j % 2][:, :],
                        in_=bass.AP(xin.tensor, (j * 128) * TPC + qc * QC,
                                    [[TPC, 128], [1, QC]])), 16)
                    if j >= 1:
                        g.wait_ge(vch.h, ytf[j - 1])
                        stc[(j - 1) % 2].inc(g.dma_start(
                            out=bass.AP(xout.tensor,
                                        ((j - 1) * 128) * TPC + qc * QC,
                                        [[TPC, 128], [1, QC]]),
                            in_=yt[(j - 1) % 2][:, :]), 16)
                g.wait_ge(vch.h, ytf[15])
                stc[15 % 2].inc(g.dma_start(
                    out=bass.AP(xout.tensor, (15 * 128) * TPC + qc * QC,
                                [[TPC, 128], [1, QC]]),
                    in_=yt[15 % 2][:, :]), 16)

    with P.block("goEnd") as block:
        @block.gpsimd
        def _(g):
            g.wait_ge(stc[0].h, stc[0].n)
            g.wait_ge(stc[1].h, stc[1].n)
    P.close()
    return P.nc
